# version 11
# baseline (speedup 1.0000x reference)
import contextlib
import numpy as np
import ml_dtypes
import concourse.bass as bass
import concourse.mybir as mybir
from concourse.bass_utils import run_bass_kernel_spmd

F32 = mybir.dt.float32
BF16 = mybir.dt.bfloat16
AF = mybir.ActivationFunctionType
ALU = mybir.AluOpType
AX = mybir.AxisListType

NCORES = 8
DM = 1024
NCH = DM // 128
H, D, R = 16, 64, 16
HQ = 4
MLP = 4096
NE = 8
FS = 2048
EPS = 1e-5
INV_DM = 1.0 / DM


def mkap(ap, dims, off=0):
    APc = type(ap)
    return APc(ap.tensor, ap.offset + off, [list(ap.ap[0])] + [list(d) for d in dims])


class Buf:
    __slots__ = ("name", "wev", "revs")

    def __init__(self, name):
        self.name = name
        self.wev = []
        self.revs = []


class EngState:
    __slots__ = ("key", "sem", "cnt", "last_wait", "q", "drained_at")

    def __init__(self, key, sem):
        self.key = key
        self.sem = sem
        self.cnt = 0
        self.last_wait = {}
        self.q = []
        self.drained_at = 0


class Stream:
    __slots__ = ("name", "sem", "cnt")

    def __init__(self, name, sem):
        self.name = name
        self.sem = sem
        self.cnt = 0


class Emitter:
    def __init__(self, nc, es):
        self.nc = nc
        self.es = es
        self.engs = {}
        for key in ("pe", "dve", "act", "pool"):
            sem = es.enter_context(nc.semaphore(f"sem_{key}"))
            self.engs[key] = EngState(key, sem)
        self.sp_q = []
        self.sp_last_wait = {}
        self.streams = {}
        self.pe_pend_r = []
        self.pe_pend_w = []

    def stream(self, name):
        if name not in self.streams:
            sem = self.es.enter_context(self.nc.semaphore(f"s_{name}"))
            self.streams[name] = Stream(name, sem)
        return self.streams[name]

    def op(self, key, fn, reads=(), writes=(), last=True):
        eng = self.engs[key]
        if key != "pe":
            for b in writes:
                assert b not in self.pe_pend_r and b not in self.pe_pend_w, \
                    f"write to {b.name} while a PE group holds a pending ref"
        deps = []
        for b in reads:
            deps.extend(b.wev)
        for b in writes:
            deps.extend(b.wev)
            deps.extend(b.revs)
        need_drain = False
        cross = {}
        for (sem, val) in deps:
            if sem is eng.sem:
                if val > eng.drained_at:
                    need_drain = True
            else:
                kk = id(sem)
                if cross.get(kk, (None, 0))[1] < val:
                    cross[kk] = (sem, val)
        for (sem, val) in cross.values():
            if eng.last_wait.get(id(sem), 0) < val:
                eng.q.append(lambda e, s=sem, v=val: e.wait_ge(s, v))
                eng.last_wait[id(sem)] = val
        if need_drain and key != "pe":
            eng.q.append(lambda e: e.drain())
            eng.drained_at = eng.cnt
        if last:
            eng.cnt += 1
            myev = (eng.sem, eng.cnt)
            eng.q.append(lambda e, f=fn, s=eng.sem: f(e).then_inc(s, 1))
            wl = list(writes) + (self.pe_pend_w if key == "pe" else [])
            rl = list(reads) + (self.pe_pend_r if key == "pe" else [])
            for b in wl:
                b.wev = [myev]
                b.revs = []
            for b in rl:
                b.revs.append(myev)
            if key == "pe":
                self.pe_pend_r = []
                self.pe_pend_w = []
            return myev
        else:
            assert key == "pe"
            eng.q.append(lambda e, f=fn: f(e))
            self.pe_pend_r.extend(reads)
            self.pe_pend_w.extend(writes)
            return None

    def dma(self, stream_name, fn, reads=(), writes=(), war=(), queue="sp"):
        st = self.stream(stream_name)
        for b in writes:
            assert b not in self.pe_pend_r and b not in self.pe_pend_w, \
                f"dma write to {b.name} while a PE group holds a pending ref"
        deps = []
        for b in reads:
            deps.extend(b.wev)
        for b in list(writes) + list(war):
            deps.extend(b.wev)
            deps.extend(b.revs)
        cross = {}
        for (sem, val) in deps:
            kk = id(sem)
            if cross.get(kk, (None, 0))[1] < val:
                cross[kk] = (sem, val)
        if queue == "sp":
            for (sem, val) in cross.values():
                if self.sp_last_wait.get(id(sem), 0) < val:
                    self.sp_q.append(lambda e, s=sem, v=val: e.wait_ge(s, v))
                    self.sp_last_wait[id(sem)] = val
            st.cnt += 16
            ev = (st.sem, st.cnt)
            self.sp_q.append(lambda e, f=fn, s=st.sem: f(e).then_inc(s, 16))
        else:
            eng = self.engs["pool" if queue == "gp" else "act"]
            for (sem, val) in cross.values():
                if eng.last_wait.get(id(sem), 0) < val:
                    eng.q.append(lambda e, s=sem, v=val: e.wait_ge(s, v))
                    eng.last_wait[id(sem)] = val
            st.cnt += 16
            ev = (st.sem, st.cnt)
            eng.q.append(lambda e, f=fn, s=st.sem: f(e).then_inc(s, 16))
        for b in writes:
            b.wev = [ev]
            b.revs = []
        for b in reads:
            b.revs.append(ev)
        return ev

    def seal(self, stream_names, bufs):
        if isinstance(stream_names, str):
            stream_names = [stream_names]
        evs = [(self.stream(n).sem, self.stream(n).cnt) for n in stream_names]
        for b in bufs:
            b.wev = list(evs)
            b.revs = []

    def barrier(self, old_bufs, new_bufs):
        evs = []
        for b in old_bufs:
            evs.extend(b.wev)
            evs.extend(b.revs)
            b.wev = []
            b.revs = []
        for b in new_bufs:
            b.revs.extend(evs)

    def sp_wait(self, ev):
        sem, val = ev
        if self.sp_last_wait.get(id(sem), 0) < val:
            self.sp_q.append(lambda e, s=sem, v=val: e.wait_ge(s, v))
            self.sp_last_wait[id(sem)] = val


def build_nc(TC):
    TB = min(512, TC)
    nblocks = TC // TB
    ntiles = TB // 128
    assert TC % TB == 0 and TB % 128 == 0

    nc = bass.Bass("TRN2", target_bir_lowering=False)

    def din(name, shape):
        return nc.dram_tensor(name, shape, F32, kind="ExternalInput").ap()

    x_in = din("x", [TC, DM])
    qw, kw, vw, ow = din("qw", [DM, DM]), din("kw", [DM, DM]), din("vw", [DM, DM]), din("ow", [DM, DM])
    fcw, pjw = din("fcw", [MLP, DM]), din("pjw", [DM, MLP])
    qb_i, kb_i, vb_i = din("qb", [1, DM]), din("kb", [1, DM]), din("vb", [1, DM])
    ob_i, fcb_i, pjb_i = din("ob", [1, DM]), din("fcb", [1, MLP]), din("pjb", [1, DM])
    l1w_i, l1b_i = din("l1w", [1, DM]), din("l1b", [1, DM])
    l2w_i, l2b_i = din("l2w", [1, DM]), din("l2b", [1, DM])
    rtw = din("rtw", [DM, NE])
    gw, uw, dwt = din("gw", [NE, DM, DM]), din("uw", [NE, DM, DM]), din("dw", [NE, DM, DM])
    sgw, suw, sdw = din("sgw", [DM, FS]), din("suw", [DM, FS]), din("sdw", [FS, DM])
    eye_i = din("eye_in", [128, 128])
    out_t = nc.dram_tensor("out", [TC, DM], F32, kind="ExternalOutput").ap()

    def dscr(name, shape):
        return nc.dram_tensor(name, shape, BF16, kind="Internal").ap()

    s_x0 = dscr("s_x0", [TC, DM])
    s_q, s_k, s_v, s_o = dscr("s_q", [DM, DM]), dscr("s_k", [DM, DM]), dscr("s_v", [DM, DM]), dscr("s_o", [DM, DM])
    s_fc, s_pj = dscr("s_fc", [MLP, DM]), dscr("s_pj", [DM, MLP])
    s_qT, s_kT = dscr("s_qT", [DM, DM]), dscr("s_kT", [DM, DM])
    s_vT, s_oT = dscr("s_vT", [DM, DM]), dscr("s_oT", [DM, DM])
    s_fcT, s_pjT = dscr("s_fcT", [DM, MLP]), dscr("s_pjT", [MLP, DM])
    s_g, s_u, s_d = dscr("s_g", [NE, DM, DM]), dscr("s_u", [NE, DM, DM]), dscr("s_d", [NE, DM, DM])
    s_sg, s_su, s_sd = dscr("s_sg", [DM, FS]), dscr("s_su", [DM, FS]), dscr("s_sd", [FS, DM])
    s_rt = dscr("s_rt", [DM, NE])
    s_at = dscr("s_at", [TC, DM])

    es = contextlib.ExitStack()
    with es:
        def sb(name, p, f, dt):
            return es.enter_context(nc.sbuf_tensor(name, [p, f], dt))

        z0 = sb("z0", 128, NCH * TB, BF16)
        zres = sb("zres", 128, NCH * TB, F32)
        xn = sb("xn", 128, NCH * TB, BF16)
        routed = sb("routed", 128, ntiles * DM, F32)
        outsb = sb("outsb", 128, DM, F32)
        x0re = sb("x0re", 128, DM, F32)
        sqb = sb("sqb", 128, 4 * TB, BF16)
        elus = sb("elus", 128, 1024, F32)
        tscr = sb("tscr", 128, TB, F32)
        atm = sb("atm", 128, 2 * DM, BF16)
        abuf = sb("abuf", 128, R * H, F32)
        abb = sb("abb", 128, R * H, BF16)
        avb = sb("avb", 128, H * D, F32)
        comb = sb("comb", 128, ntiles * NE, F32)
        csc = sb("csc", 128, 16, F32)
        eye = sb("eye", 128, 128, F32)
        ones_b = sb("ones_b", 1, TB, BF16)
        onec_b = sb("onec_b", 128, 1, BF16)
        epsr = sb("epsr", 1, 1, F32)
        lnv = sb("lnv", 128, 4 * NCH, F32)
        bvec = sb("bvec", 128, 2 * NCH + MLP // 128, F32)
        qkvb = sb("qkvb", 1, 3 * DM, BF16)
        wrt = sb("wrt", 128, NCH * NE, BF16)
        rows_f = sb("rows_f", 1, 4 * TB, F32)
        rows_b = sb("rows_b", 1, 6 * TB, BF16)
        UN = 45056
        una = sb("una", 128, UN, BF16)

        CARVE = {
            "mid": (0, 4096), "mids": (4096, 2048),
            "pool0": (6144, 4096), "pool1": (10240, 4096), "pool2": (14336, 4096),
            "qtm": (18432, 2048), "kdup": (20480, 4096), "vdup": (24576, 4096),
            "wqvA": (28672, 4096), "wqvB": (32768, 4096), "p3b": (36864, 4096),
            "y": (40960, 4096),
            "attnf": (0, 4096),
            "hbuf": (0, 16384), "wfc0": (16384, 4096), "wfc1": (20480, 4096),
            "wpj0": (24576, 8192), "wpj1": (32768, 8192),
        }

        def carve(nm):
            o, s = CARVE[nm]
            return una[:, o:o + s]

        psum = [es.enter_context(nc.psum_tensor(f"ps{i}", [128, 512], F32)) for i in range(8)]

        em = Emitter(nc, es)
        Bufs = {}

        def bg(name):
            if name not in Bufs:
                Bufs[name] = Buf(name)
            return Bufs[name]

        G_CD = [bg(n) for n in ("mid", "mids", "pool0", "pool1", "pool2", "qtm",
                                 "kdup", "vdup", "wqvA", "wqvB", "p3b")]
        G_E = [bg("attnf")]
        G_G = [bg(n) for n in ("hbuf", "wfc0", "wfc1", "wpj0", "wpj1")]

        em.op("dve", lambda e: e.memset(ones_b[:], 1.0), writes=[bg("ones_b")])
        em.op("dve", lambda e: e.memset(onec_b[:], 1.0), writes=[bg("onec_b")])
        em.op("dve", lambda e: e.memset(epsr[:], EPS), writes=[bg("epsr")])

        em.dma("cx", lambda e: e.dma_start(out=s_x0[:, :], in_=x_in[:, :]), queue="gp")
        em.seal("cx", [bg("s_x0")])
        for nm, src, dst in (("s_q", qw, s_q), ("s_k", kw, s_k), ("s_v", vw, s_v),
                             ("s_o", ow, s_o), ("s_fc", fcw, s_fc), ("s_pj", pjw, s_pj)):
            em.dma("c" + nm, lambda e, s=src, d=dst: e.dma_start(out=d[:, :], in_=s[:, :]), queue="gp")
            em.seal("c" + nm, [bg(nm)])
        em.dma("crt", lambda e: e.dma_start(out=s_rt[:, :], in_=rtw[:, :]), queue="gp")
        em.seal("crt", [bg("s_rt")])
        for ei in range(NE):
            em.dma("cmoe", lambda e, i=ei: e.dma_start(out=s_g[i, :, :], in_=gw[i, :, :]), queue="gp")
            em.dma("cmoe", lambda e, i=ei: e.dma_start(out=s_u[i, :, :], in_=uw[i, :, :]), queue="gp")
            em.dma("cmoe", lambda e, i=ei: e.dma_start(out=s_d[i, :, :], in_=dwt[i, :, :]), queue="gp")
        em.dma("cmoe", lambda e: e.dma_start(out=s_sg[:, :], in_=sgw[:, :]), queue="gp")
        em.dma("cmoe", lambda e: e.dma_start(out=s_su[:, :], in_=suw[:, :]), queue="gp")
        em.dma("cmoe", lambda e: e.dma_start(out=s_sd[:, :], in_=sdw[:, :]), queue="gp")
        em.seal("cmoe", [bg("s_g"), bg("s_u"), bg("s_d"), bg("s_sg"), bg("s_su"), bg("s_sd")])
        for i, src in enumerate((qb_i, kb_i, vb_i)):
            em.dma("cqb", lambda e, s=src, o=i * DM: e.dma_start(out=qkvb[:, o:o + DM], in_=s[:, :]),
                   queue="gp")
        em.seal("cqb", [bg("qkvb")])

        p3b_t = carve("p3b")
        stg_ap = {0: p3b_t, 1: carve("kdup")}

        stg_ap[2] = carve("vdup")

        def prep_T(nm, s_src, s_dst, owid, ichunks):
            for ic in range(ichunks):
                sl = ic % 3
                stg = stg_ap[sl][:, 0:owid]
                sbuf_b = bg(("p3b", "kdup", "vdup")[sl])
                em.dma(f"tp{sl}", lambda e, s=s_src, i=ic, g=stg, ow=owid: e.dma_start(
                    out=g, in_=s[0:ow, i * 128:(i + 1) * 128], transpose=True),
                    reads=[bg(nm)], writes=[sbuf_b])
                em.dma(f"tw{nm}", lambda e, d=s_dst, i=ic, g=stg, ow=owid: e.dma_start(
                    out=d[i * 128:(i + 1) * 128, 0:ow], in_=g),
                    reads=[sbuf_b], writes=[bg(nm + "T")])
            em.seal(f"tw{nm}", [bg(nm + "T")])

        prep_T("s_q", s_q, s_qT, DM, NCH)
        prep_T("s_k", s_k, s_kT, DM, NCH)
        prep_T("s_v", s_v, s_vT, DM, NCH)
        prep_T("s_o", s_o, s_oT, DM, NCH)
        prep_T("s_fc", s_fc, s_fcT, MLP, NCH)
        prep_T("s_pj", s_pj, s_pjT, DM, MLP // 128)


        em.dma("seye", lambda e: e.dma_start(out=eye[:], in_=eye_i[:, :]))
        em.seal("seye", [bg("eye")])
        for vi, src in enumerate((l1w_i, l1b_i, l2w_i, l2b_i)):
            for c in range(NCH):
                em.dma("slnv", lambda e, s=src, v=vi, cc=c: e.dma_start(
                    out=lnv[:, v * NCH + cc: v * NCH + cc + 1],
                    in_=s[0, cc * 128:(cc + 1) * 128].unsqueeze(1)))
        em.seal("slnv", [bg("lnv")])
        for base, src, n in ((0, ob_i, NCH), (NCH, pjb_i, NCH), (2 * NCH, fcb_i, MLP // 128)):
            for c in range(n):
                em.dma("sbv", lambda e, s=src, b=base, cc=c: e.dma_start(
                    out=bvec[:, b + cc: b + cc + 1],
                    in_=s[0, cc * 128:(cc + 1) * 128].unsqueeze(1)))
        em.seal("sbv", [bg("bvec")])
        for c in range(NCH):
            em.dma("lrt", lambda e, cc=c: e.dma_start(
                out=wrt[:, cc * NE:(cc + 1) * NE], in_=s_rt[cc * 128:(cc + 1) * 128, :]),
                reads=[bg("s_rt")])
        em.seal("lrt", [bg("wrt")])


        def MM(out_ap, lh_ap, rh_ap, st_, sp_):
            return lambda e: e.matmul(out=out_ap, lhsT=lh_ap, rhs=rh_ap, start=st_, stop=sp_)

        def ln_stats(src_sb, src_buf, src_is_f32, want_s2):
            b7, b6 = bg("ps7"), bg("ps6")
            for c in range(NCH):
                sl = src_sb[:, c * TB:(c + 1) * TB]
                if src_is_f32:
                    ss = c % 2
                    xb = sqb[:, ss * TB:(ss + 1) * TB]
                    em.op("dve", lambda e, s=sl, d=xb: e.tensor_copy(out=d, in_=s),
                          reads=[src_buf], writes=[bg(f"sqb{ss}")])
                    xb_b = bg(f"sqb{ss}")
                else:
                    xb = sl
                    xb_b = src_buf
                em.op("pe", MM(psum[7][0:1, 0:TB], onec_b[:], xb, c == 0, c == NCH - 1),
                      reads=[bg("onec_b"), xb_b], writes=[b7], last=True)
            for c in range(NCH):
                if src_is_f32:
                    sl = src_sb[:, c * TB:(c + 1) * TB]
                    ss = 2 + c % 2
                    sq = sqb[:, ss * TB:(ss + 1) * TB]
                    em.op("act", lambda e, s=sl, d=sq: e.activation(out=d, in_=s, func=AF.Square),
                          reads=[src_buf], writes=[bg(f"sqb{ss}")])
                    sq_b = bg(f"sqb{ss}")
                else:
                    sl = src_sb[:, c * TB:(c + 1) * TB]
                    ss = 2 + c % 2
                    sq = sqb[:, ss * TB:(ss + 1) * TB]
                    em.op("dve", lambda e, s=sl, d=sq: e.tensor_tensor(out=d, in0=s, in1=s, op=ALU.mult),
                          reads=[src_buf], writes=[bg(f"sqb{ss}")])
                    sq_b = bg(f"sqb{ss}")
                em.op("pe", MM(psum[6][0:1, 0:TB], onec_b[:], sq, c == 0, c == NCH - 1),
                      reads=[bg("onec_b"), sq_b], writes=[b6], last=True)
            rf, rb = bg("rows_f"), bg("rows_b")
            m_r = rows_f[:, 0:TB]
            e2_r = rows_f[:, TB:2 * TB]
            s_r = rows_f[:, 2 * TB:3 * TB]
            s2_r = rows_f[:, 3 * TB:4 * TB]
            em.op("dve", lambda e: e.tensor_scalar(out=m_r, in0=psum[7][0:1, 0:TB], scalar1=INV_DM,
                                                   scalar2=None, op0=ALU.mult),
                  reads=[b7], writes=[rf])
            em.op("dve", lambda e: e.tensor_scalar(out=e2_r, in0=psum[6][0:1, 0:TB], scalar1=INV_DM,
                                                   scalar2=None, op0=ALU.mult),
                  reads=[b6], writes=[rf])
            em.op("dve", lambda e: e.tensor_tensor(out=s_r, in0=m_r, in1=m_r, op=ALU.mult),
                  reads=[rf], writes=[rf])
            em.op("dve", lambda e: e.tensor_tensor(out=s_r, in0=e2_r, in1=s_r, op=ALU.subtract),
                  reads=[rf], writes=[rf])
            em.op("act", lambda e: e.activation(out=s_r, in_=s_r, func=AF.Sqrt, bias=epsr[:, 0:1]),
                  reads=[rf, bg("epsr")], writes=[rf])
            em.op("dve", lambda e: e.reciprocal(out=s_r, in_=s_r), reads=[rf], writes=[rf])
            if want_s2:
                em.op("dve", lambda e: e.tensor_tensor(out=s2_r, in0=s_r, in1=s_r, op=ALU.mult),
                      reads=[rf], writes=[rf])
                em.op("dve", lambda e: e.tensor_scalar(out=s2_r, in0=s2_r, scalar1=-EPS,
                                                       scalar2=1.0 + EPS, op0=ALU.mult, op1=ALU.add),
                      reads=[rf], writes=[rf])
                em.op("act", lambda e: e.activation(out=s2_r, in_=s2_r, func=AF.Sqrt),
                      reads=[rf], writes=[rf])
                em.op("dve", lambda e: e.reciprocal(out=s2_r, in_=s2_r), reads=[rf], writes=[rf])
                em.op("dve", lambda e: e.tensor_tensor(out=s2_r, in0=s2_r, in1=s_r, op=ALU.mult),
                      reads=[rf], writes=[rf])
            outs = {}
            stats = [("m", 0, m_r), ("s", 2, s_r)] + ([("s3", 4, s2_r)] if want_s2 else [])
            for (snm, bo, row) in stats:
                hi = rows_b[:, bo * TB:(bo + 1) * TB]
                lo = rows_b[:, (bo + 1) * TB:(bo + 2) * TB]
                em.op("dve", lambda e, r=row, h_=hi: e.tensor_copy(out=h_, in_=r),
                      reads=[rf], writes=[rb])
                em.op("dve", lambda e, r=row, h_=hi, l_=lo: e.tensor_tensor(
                    out=l_, in0=r, in1=h_, op=ALU.subtract),
                    reads=[rf, rb], writes=[rb])
                outs[snm] = (hi, lo)
            return outs

        def bcast(rowpair, pidx):
            hi, lo = rowpair
            bp = bg(f"ps{pidx}")
            em.op("pe", MM(psum[pidx][:, 0:TB], ones_b[0:1, 0:128], hi, True, False),
                  reads=[bg("ones_b"), bg("rows_b")], writes=[bp], last=False)
            em.op("pe", MM(psum[pidx][:, 0:TB], ones_b[0:1, 0:128], lo, False, True),
                  reads=[bg("ones_b"), bg("rows_b")], writes=[bp], last=True)
            return bp

        bias_roff = {"q": 0, "k": DM, "v": 2 * DM}
        qtm_c, kdup_c, vdup_c = carve("qtm"), carve("kdup"), carve("vdup")
        attnf_c = carve("attnf")
        wqv_c = [carve("wqvA"), carve("wqvB")]
        hbuf_c = carve("hbuf")
        wfc_c = [carve("wfc0"), carve("wfc1")]
        wpj_c = [carve("wpj0"), carve("wpj1")]
        mid_c, mids_c = carve("mid"), carve("mids")
        pool_c = [carve("pool0"), carve("pool1"), carve("pool2")]
        y_c = carve("y")
        pool_seq = [0]
        wqv_seq = [0]

        def wqv_load(s_wT, src_buf_name, half):
            sl = wqv_seq[0] % 2
            wqv_seq[0] += 1
            wsl = wqv_c[sl]
            bw = bg("wqv" + "AB"[sl])
            em.dma(f"lwqv{sl}", lambda e, d=wsl, s=s_wT, o=half * 512: e.dma_start(
                out=d.rearrange("p (c n) -> p c n", c=NCH, n=512),
                in_=s.rearrange("(c p) n -> p c n", c=NCH, p=128)[:, :, o:o + 512]),
                reads=[bg(src_buf_name)], writes=[bw])
            return wsl, bw

        def emit_A(blk):
            t0 = blk * TB
            bz0 = bg("z0")
            for c in range(NCH):
                em.dma("lz0", lambda e, cc=c, tt=t0: e.dma_start(
                    out=z0[:, cc * TB:(cc + 1) * TB],
                    in_=s_x0[tt:tt + TB, cc * 128:(cc + 1) * 128], transpose=True),
                    reads=[bg("s_x0")], war=([bz0] if c == 0 else []))
            em.seal("lz0", [bz0])
            return ln_stats(z0, bz0, False, True)

        def emit_Bp(blk, st1):
            bm = bcast(st1["m"], 4)
            bs3 = bcast(st1["s3"], 6)
            bz0, bxn = bg("z0"), bg("xn")
            for c in range(NCH):
                z0c = z0[:, c * TB:(c + 1) * TB]
                xnc = xn[:, c * TB:(c + 1) * TB]
                t_ = tscr[:, 0:TB]
                em.op("dve", lambda e, a=t_, b=z0c: e.tensor_tensor(
                    out=a, in0=b, in1=psum[4][:, 0:TB], op=ALU.subtract),
                    reads=[bz0, bm], writes=[bg("tscr")])
                em.op("dve", lambda e, a=xnc, b=t_, cc=c: e.scalar_tensor_tensor(
                    out=a, in0=b, scalar=lnv[:, cc:cc + 1], in1=psum[6][:, 0:TB],
                    op0=ALU.mult, op1=ALU.mult),
                    reads=[bg("tscr"), bs3, bg("lnv")], writes=[bxn])
                em.op("dve", lambda e, a=xnc, cc=c: e.tensor_scalar(
                    out=a, in0=a, scalar1=lnv[:, NCH + cc:NCH + cc + 1], scalar2=None,
                    op0=ALU.add),
                    reads=[bg("lnv")], writes=[bxn])

        def make_C_pair(blk, pr):
            def unit():
                jlist = [j for j in (2 * pr, 2 * pr + 1) if j < ntiles]
                bxn = bg("xn")
                for pname, s_wT in (("q", s_qT), ("k", s_kT), ("v", s_vT)):
                    for half in range(2):
                        wsl, bw = wqv_load(s_wT, "s_" + pname + "T", half)
                        o0 = half * 512
                        for j in jlist:
                            jl = j - 2 * pr
                            pidx = (0, 1, 4, 5)[jl * 2 + half]
                            php = psum[pidx][:, 0:512]
                            bp = bg(f"ps{pidx}")
                            for c in range(NCH):
                                lh = xn[:, c * TB + j * 128: c * TB + (j + 1) * 128]
                                rh = wsl[:, c * 512:(c + 1) * 512]
                                em.op("pe", MM(php, lh, rh, c == 0, False),
                                      reads=[bxn, bw], writes=[bp], last=False)
                            bo = bias_roff[pname]
                            em.op("pe", MM(php, ones_b[0:1, 0:128],
                                           qkvb[:, bo + o0: bo + o0 + 512], False, True),
                                  reads=[bg("ones_b"), bg("qkvb")], writes=[bp], last=True)
                            if pname == "v":
                                d1 = vdup_c[:, jl * 2 * DM + o0: jl * 2 * DM + o0 + 512]
                                d2 = vdup_c[:, jl * 2 * DM + DM + o0: jl * 2 * DM + DM + o0 + 512]
                                bdst = bg("vdup")
                                em.op("act", lambda e, d=d1, p=php: e.activation(out=d, in_=p, func=AF.Copy),
                                      reads=[bp], writes=[bdst])
                                em.op("act", lambda e, d=d2, s=d1: e.activation(out=d, in_=s, func=AF.Copy),
                                      reads=[bdst], writes=[bdst])
                            else:
                                r_s = elus[:, 0:512]
                                e_s = elus[:, 512:1024]
                                bel = bg("elus")
                                em.op("act", lambda e, d=r_s, p=php: e.activation(
                                    out=d, in_=p, func=AF.Relu, scale=-1.0),
                                    reads=[bp], writes=[bel])
                                em.op("act", lambda e, d=e_s, s=r_s: e.activation(
                                    out=d, in_=s, func=AF.Exp, scale=-1.0),
                                    reads=[bel], writes=[bel])
                                if pname == "q":
                                    dd = qtm_c[:, jl * DM + o0: jl * DM + o0 + 512]
                                    bdst = bg("qtm")
                                else:
                                    dd = kdup_c[:, jl * 2 * DM + o0: jl * 2 * DM + o0 + 512]
                                    bdst = bg("kdup")
                                em.op("dve", lambda e, d=dd, s=e_s, p=php: e.scalar_tensor_tensor(
                                    out=d, in0=p, scalar=0.0, in1=s, op0=ALU.max, op1=ALU.add),
                                    reads=[bp, bel], writes=[bdst])
                                if pname == "k":
                                    d2 = kdup_c[:, jl * 2 * DM + DM + o0: jl * 2 * DM + DM + o0 + 512]
                                    em.op("act", lambda e, d=d2, s=dd: e.activation(
                                        out=d, in_=s, func=AF.Copy),
                                        reads=[bdst], writes=[bdst])
            return unit

        def make_D_units(blk, j, pr):
            t0 = blk * TB
            jl = j - 2 * pr
            q_ap = qtm_c[:, jl * DM:(jl + 1) * DM]
            k_ap = kdup_c[:, jl * 2 * DM:(jl + 1) * 2 * DM]
            v_ap = vdup_c[:, jl * 2 * DM:(jl + 1) * 2 * DM]

            def a_q(hq):
                def unit():
                    ba, bp3 = bg("abuf"), bg("p3b")
                    bq, bk = bg("qtm"), bg("kdup")
                    ho = hq * HQ
                    em.op("dve", lambda e, q=q_ap, k=k_ap, o=ho: e.tensor_tensor(
                        out=mkap(p3b_t, [[HQ * D, R], [1, HQ * D]]),
                        in0=mkap(q, [[0, R], [1, HQ * D]], off=o * D),
                        in1=mkap(k, [[D, R], [1, HQ * D]], off=o * D),
                        op=ALU.mult),
                        reads=[bq, bk], writes=[bp3])
                    em.op("dve", lambda e, o=ho: e.tensor_reduce(
                        out=mkap(abuf[:], [[H, R], [1, HQ]], off=o),
                        in_=mkap(p3b_t, [[HQ * D, R], [D, HQ], [1, D]]),
                        axis=AX.X, op=ALU.add),
                        reads=[bp3], writes=[ba])
                return unit

            def a_tail():
                def unit():
                    ba, bab, bc = bg("abuf"), bg("abb"), bg("csc")
                    em.op("dve", lambda e: e.tensor_reduce(
                        out=mkap(csc[:], [[1, H]]),
                        in_=mkap(abuf[:], [[1, H], [H, R]]),
                        axis=AX.X, op=ALU.add),
                        reads=[ba], writes=[bc])
                    em.op("dve", lambda e: e.reciprocal(out=csc[:, 0:H], in_=csc[:, 0:H]),
                          reads=[bc], writes=[bc])
                    em.op("act", lambda e: e.activation(out=abb[:], in_=abuf[:], func=AF.Copy),
                          reads=[ba], writes=[bab])
                return unit

            def av_q(hq):
                def unit():
                    bab, bav, bv, bp3 = bg("abb"), bg("avb"), bg("vdup"), bg("p3b")
                    ho = hq * HQ
                    em.op("dve", lambda e, v=v_ap, o=ho: e.tensor_tensor(
                        out=mkap(p3b_t, [[D * R, HQ], [R, D], [1, R]]),
                        in0=mkap(abb[:], [[1, HQ], [0, D], [H, R]], off=o),
                        in1=mkap(v, [[D, HQ], [1, D], [D, R]], off=o * D),
                        op=ALU.mult),
                        reads=[bab, bv], writes=[bp3])
                    em.op("dve", lambda e, o=ho: e.tensor_reduce(
                        out=mkap(avb[:], [[D, HQ], [1, D]], off=o * D),
                        in_=mkap(p3b_t, [[D * R, HQ], [R, D], [1, R]]),
                        axis=AX.X, op=ALU.add),
                        reads=[bp3], writes=[bav])
                return unit

            def d_tail():
                def unit():
                    bav, bc = bg("avb"), bg("csc")
                    asl = j % 2
                    a_ap = atm[:, asl * DM:(asl + 1) * DM]
                    bat = bg(f"atm{asl}")
                    em.op("dve", lambda e, d=a_ap: e.tensor_tensor(
                        out=mkap(d, [[D, H], [1, D]]),
                        in0=mkap(avb[:], [[D, H], [1, D]]),
                        in1=mkap(csc[:], [[1, H], [0, D]]),
                        op=ALU.mult),
                        reads=[bav, bc], writes=[bat])
                    em.dma(f"sat{asl}", lambda e, d=a_ap, jj=j, tt=t0: e.dma_start(
                        out=s_at[tt + jj * 128: tt + (jj + 1) * 128, :], in_=d),
                        reads=[bat], writes=[bg(f"s_at{asl}")])
                return unit

            return ([a_q(h) for h in range(4)] + [a_tail()] +
                    [av_q(h) for h in range(4)] + [d_tail()])

        def make_gu_unit(blk, ei, mat, s_src, half):
            def unit():
                sl = pool_seq[0] % 3
                pool_seq[0] += 1
                dst = pool_c[sl]
                bw = bg(f"pool{sl}")
                em.dma(f"lpool{sl}", lambda e, d=dst, s=s_src, i=ei, o=half * 512: e.dma_start(
                    out=d.rearrange("p (c n) -> p c n", c=NCH, n=512),
                    in_=s[i].rearrange("(c p) n -> p c n", c=NCH, p=128)[:, :, o:o + 512]),
                    reads=[bg("s_" + mat)], writes=[bw])
                by = bg("y")
                for oc in range(4):
                    co = half * 4 + oc
                    pidx = (0, 1, 4, 5)[co % 4]
                    php = psum[pidx][:, 0:TB]
                    bp = bg(f"ps{pidx}")
                    for c in range(NCH):
                        em.op("pe", MM(php, dst[:, c * 512 + oc * 128: c * 512 + (oc + 1) * 128],
                                       y_c[:, c * TB:(c + 1) * TB], c == 0, c == NCH - 1),
                              reads=[bw, by], writes=[bp], last=(c == NCH - 1))
                    mc = mid_c[:, co * TB:(co + 1) * TB]
                    if mat == "g":
                        em.op("act", lambda e, d=mc, p=php: e.activation(out=d, in_=p, func=AF.Silu),
                              reads=[bp], writes=[bg("mid")])
                    else:
                        usb = sqb[:, (co % 4) * TB:(co % 4 + 1) * TB]
                        bub = bg(f"sqb{co % 4}")
                        em.op("act", lambda e, d=usb, p=php: e.activation(out=d, in_=p, func=AF.Copy),
                              reads=[bp], writes=[bub])
                        em.op("pool", lambda e, d=mc, s=usb: e.tensor_tensor(
                            out=d, in0=d, in1=s, op=ALU.mult),
                            reads=[bub, bg("mid")], writes=[bg("mid")])
            return unit

        def make_down_unit(blk, ei, half):
            def unit():
                sl = pool_seq[0] % 3
                pool_seq[0] += 1
                dst = pool_c[sl]
                bw = bg(f"pool{sl}")
                em.dma(f"lpool{sl}", lambda e, d=dst, i=ei, o=half * 512: e.dma_start(
                    out=d.rearrange("p (c n) -> p c n", c=NCH, n=512),
                    in_=s_d[i].rearrange("(c p) n -> p c n", c=NCH, p=128)[:, :, o:o + 512]),
                    reads=[bg("s_d")], writes=[bw])
                o0 = half * 512
                for j in range(ntiles):
                    pidx = (2, 6)[j % 2] + half
                    php = psum[pidx][:, 0:512]
                    bp = bg(f"ps{pidx}")
                    for c in range(NCH):
                        em.op("pe", MM(php, mid_c[:, c * TB + j * 128: c * TB + (j + 1) * 128],
                                       dst[:, c * 512:(c + 1) * 512], c == 0, c == NCH - 1),
                              reads=[bg("mid"), bw], writes=[bp], last=(c == NCH - 1))
                    rsl = routed[:, j * DM + o0: j * DM + o0 + 512]
                    if ei == 0:
                        em.op("dve", lambda e, d=rsl, p=php, jj=j, i=ei: e.tensor_scalar(
                            out=d, in0=p, scalar1=comb[:, jj * NE + i: jj * NE + i + 1],
                            scalar2=None, op0=ALU.mult),
                            reads=[bp, bg("comb")], writes=[bg("routed")])
                    else:
                        em.op("dve", lambda e, d=rsl, p=php, jj=j, i=ei: e.scalar_tensor_tensor(
                            out=d, in0=p, scalar=comb[:, jj * NE + i: jj * NE + i + 1],
                            in1=d, op0=ALU.mult, op1=ALU.add),
                            reads=[bp, bg("comb"), bg("routed")], writes=[bg("routed")])
            return unit

        def make_lat_chunk(blk, c):
            def unit():
                t0 = blk * TB
                if c == 0:
                    em.barrier([bg("mid")], [bg("attnf")])
                em.dma("lat", lambda e, cc=c, tt=t0: e.dma_start(
                    out=attnf_c[:, cc * TB:(cc + 1) * TB],
                    in_=s_at[tt:tt + TB, cc * 128:(cc + 1) * 128], transpose=True),
                    reads=[bg("s_at0"), bg("s_at1")],
                    war=([bg("attnf")] if c == 0 else []), queue="act")
                if c == NCH - 1:
                    em.seal("lat", [bg("attnf")])
            return unit

        def make_shared_gu_unit(blk, qt):
            def unit():
                by = bg("y")
                slots = {}
                for nm, s_src in (("sg", s_sg), ("su", s_su)):
                    sl = pool_seq[0] % 3
                    pool_seq[0] += 1
                    slots[nm] = (pool_c[sl], bg(f"pool{sl}"))
                    em.dma(f"lpool{sl}", lambda e, d=pool_c[sl], s=s_src, q=qt: e.dma_start(
                        out=d.rearrange("p (c n) -> p c n", c=NCH, n=512),
                        in_=s.rearrange("(c p) n -> p c n", c=NCH, p=128)[:, :, q * 512:(q + 1) * 512]),
                        reads=[bg("s_" + nm)], writes=[bg(f"pool{sl}")])
                for nm in ("sg", "su"):
                    dst, bw = slots[nm]
                    for oc in range(4):
                        pidx = (0, 1, 4, 5)[oc % 4]
                        php = psum[pidx][:, 0:TB]
                        bp = bg(f"ps{pidx}")
                        for c in range(NCH):
                            em.op("pe", MM(php, dst[:, c * 512 + oc * 128: c * 512 + (oc + 1) * 128],
                                           y_c[:, c * TB:(c + 1) * TB], c == 0, c == NCH - 1),
                                  reads=[bw, by], writes=[bp], last=(c == NCH - 1))
                        mc = mids_c[:, oc * TB:(oc + 1) * TB]
                        if nm == "sg":
                            em.op("act", lambda e, d=mc, p=php: e.activation(out=d, in_=p, func=AF.Silu),
                                  reads=[bp], writes=[bg("mids")])
                        else:
                            usb = sqb[:, (oc % 4) * TB:(oc % 4 + 1) * TB]
                            bub = bg(f"sqb{oc % 4}")
                            em.op("act", lambda e, d=usb, p=php: e.activation(out=d, in_=p, func=AF.Copy),
                                  reads=[bp], writes=[bub])
                            em.op("pool", lambda e, d=mc, s=usb: e.tensor_tensor(
                                out=d, in0=d, in1=s, op=ALU.mult),
                                reads=[bub, bg("mids")], writes=[bg("mids")])
            return unit

        def make_shared_down_unit(blk, qt):
            def unit():
                sl = pool_seq[0] % 3
                pool_seq[0] += 1
                dst = pool_c[sl]
                bw = bg(f"pool{sl}")
                em.dma(f"lpool{sl}", lambda e, d=dst, q=qt: e.dma_start(
                    out=d.rearrange("p (c n) -> p c n", c=4, n=DM),
                    in_=s_sd.rearrange("(c p) n -> p c n", c=FS // 128, p=128)[:, q * 4:(q + 1) * 4, :]),
                    reads=[bg("s_sd")], writes=[bw])
                for j in range(ntiles):
                    for half in range(2):
                        pidx = (2, 6)[j % 2] + half
                        php = psum[pidx][:, 0:512]
                        bp = bg(f"ps{pidx}")
                        o0 = half * 512
                        for c in range(4):
                            em.op("pe", MM(php, mids_c[:, c * TB + j * 128: c * TB + (j + 1) * 128],
                                           dst[:, c * DM + o0: c * DM + o0 + 512], c == 0, c == 3),
                                  reads=[bg("mids"), bw], writes=[bp], last=(c == 3))
                        rsl = routed[:, j * DM + o0: j * DM + o0 + 512]
                        em.op("dve", lambda e, d=rsl, p=php: e.tensor_tensor(
                            out=d, in0=d, in1=p, op=ALU.add),
                            reads=[bp, bg("routed")], writes=[bg("routed")])
            return unit

        def make_L_unit(blk, j):
            def unit():
                t0 = blk * TB
                bxr = bg("x0re")
                em.dma("lx0", lambda e, jj=j, tt=t0: e.dma_start(
                    out=x0re[:], in_=x_in[tt + jj * 128: tt + (jj + 1) * 128, :]),
                    reads=[], writes=[bxr])
                bout = bg("outsb")
                for half in range(2):
                    pidx = 2 + half
                    bp = bg(f"ps{pidx}")
                    for cc in range(4):
                        co = half * 4 + cc
                        em.op("pe", lambda e, o=psum[pidx][:, cc * 128:(cc + 1) * 128],
                              i_=zres[:, co * TB + j * 128: co * TB + (j + 1) * 128]: e.transpose(
                                  out=o, in_=i_, identity=eye[:]),
                              reads=[bg("zres"), bg("eye")], writes=[bp], last=(cc == 3))
                    osl = outsb[:, half * 512: half * 512 + 512]
                    em.op("dve", lambda e, d=osl, p=psum[pidx][:, 0:512], jj=j, h=half: e.tensor_tensor(
                        out=d, in0=p, in1=routed[:, jj * DM + h * 512: jj * DM + h * 512 + 512],
                        op=ALU.add),
                        reads=[bp, bg("routed")], writes=[bout])
                    em.op("dve", lambda e, d=osl, h=half: e.tensor_tensor(
                        out=d, in0=d, in1=x0re[:, h * 512: h * 512 + 512], op=ALU.add),
                        reads=[bxr, bout], writes=[bout])
                em.dma("sout", lambda e, jj=j, tt=t0: e.dma_start(
                    out=out_t[tt + jj * 128: tt + (jj + 1) * 128, :], in_=outsb[:]),
                    reads=[bg("outsb")], writes=[])
            return unit

        def build_tail(blk):
            units = []
            for ei in range(NE):
                for mat, s_src in (("g", s_g), ("u", s_u)):
                    for half in range(2):
                        units.append(make_gu_unit(blk, ei, mat, s_src, half))
                for half in range(2):
                    units.append(make_down_unit(blk, ei, half))
            for qt in range(FS // 512):
                units.append(make_shared_gu_unit(blk, qt))
                units.append(make_shared_down_unit(blk, qt))
            for j in range(ntiles):
                units.append(make_L_unit(blk, j))
            return units

        def emit_serial_tailstages(blk):
            bm = bcast((rows_b[:, 0:TB], rows_b[:, TB:2 * TB]), 4)
            bs = bcast((rows_b[:, 2 * TB:3 * TB], rows_b[:, 3 * TB:4 * TB]), 5)
            bz0, bz = bg("z0"), bg("zres")
            for c in range(NCH):
                zc = zres[:, c * TB:(c + 1) * TB]
                z0c = z0[:, c * TB:(c + 1) * TB]
                em.op("dve", lambda e, a=zc, b=z0c: e.tensor_tensor(
                    out=a, in0=b, in1=psum[4][:, 0:TB], op=ALU.subtract),
                    reads=[bz0, bm], writes=[bz])
                em.op("dve", lambda e, a=zc: e.tensor_tensor(
                    out=a, in0=a, in1=psum[5][:, 0:TB], op=ALU.mult),
                    reads=[bs, bz], writes=[bz])
            batf = bg("attnf")
            for half in range(2):
                wsl, bw = wqv_load(s_oT, "s_oT", half)
                for oc in range(4):
                    co = half * 4 + oc
                    pidx = (0, 1, 4, 5)[co % 4]
                    php = psum[pidx][:, 0:TB]
                    bp = bg(f"ps{pidx}")
                    for c in range(NCH):
                        lh = wsl[:, c * 512 + oc * 128: c * 512 + (oc + 1) * 128]
                        rh = attnf_c[:, c * TB:(c + 1) * TB]
                        em.op("pe", MM(php, lh, rh, c == 0, c == NCH - 1),
                              reads=[bw, batf], writes=[bp], last=(c == NCH - 1))
                    zc = zres[:, co * TB:(co + 1) * TB]
                    em.op("dve", lambda e, a=zc, p=php, cc=co: e.scalar_tensor_tensor(
                        out=a, in0=p, scalar=bvec[:, cc:cc + 1], in1=a, op0=ALU.add, op1=ALU.add),
                        reads=[bp, bg("bvec"), bg("zres")], writes=[bg("zres")])
            st2 = ln_stats(zres, bg("zres"), True, False)
            bm2 = bcast(st2["m"], 4)
            bs2_ = bcast(st2["s"], 5)
            for c in range(NCH):
                zc = zres[:, c * TB:(c + 1) * TB]
                xnc = xn[:, c * TB:(c + 1) * TB]
                em.op("dve", lambda e, a=tscr[:, 0:TB], b=zc: e.tensor_tensor(
                    out=a, in0=b, in1=psum[4][:, 0:TB], op=ALU.subtract),
                    reads=[bg("zres"), bm2], writes=[bg("tscr")])
                em.op("dve", lambda e, a=xnc, b=tscr[:, 0:TB], cc=c: e.scalar_tensor_tensor(
                    out=a, in0=b, scalar=lnv[:, 2 * NCH + cc: 2 * NCH + cc + 1],
                    in1=psum[5][:, 0:TB], op0=ALU.mult, op1=ALU.mult),
                    reads=[bg("tscr"), bs2_, bg("lnv")], writes=[bg("xn")])
                em.op("dve", lambda e, a=xnc, cc=c: e.tensor_scalar(
                    out=a, in0=a, scalar1=lnv[:, 3 * NCH + cc: 3 * NCH + cc + 1],
                    scalar2=None, op0=ALU.add),
                    reads=[bg("lnv")], writes=[bg("xn")])
            em.barrier(G_CD + G_E, G_G)
            for g in range(MLP // 512):
                slot = g % 2
                wsl = wfc_c[slot]
                bw = bg(f"wfc{slot}")
                em.dma(f"lwfc{slot}", lambda e, d=wsl, gg=g: e.dma_start(
                    out=d.rearrange("p (c n) -> p c n", c=NCH, n=512),
                    in_=s_fcT.rearrange("(c p) n -> p c n", c=NCH, p=128)[:, :, gg * 512:(gg + 1) * 512]),
                    reads=[bg("s_fcT")], writes=[bw])
                for oc in range(4):
                    co = g * 4 + oc
                    pidx = (0, 1, 4, 5)[co % 4]
                    php = psum[pidx][:, 0:TB]
                    bp = bg(f"ps{pidx}")
                    for c in range(NCH):
                        lh = wsl[:, c * 512 + oc * 128: c * 512 + (oc + 1) * 128]
                        rh = xn[:, c * TB:(c + 1) * TB]
                        em.op("pe", MM(php, lh, rh, c == 0, c == NCH - 1),
                              reads=[bw, bg("xn")], writes=[bp], last=(c == NCH - 1))
                    hc = hbuf_c[:, co * TB:(co + 1) * TB]
                    em.op("act", lambda e, d=hc, p=php, cc=co: e.activation(
                        out=d, in_=p, func=AF.Gelu, bias=bvec[:, 2 * NCH + cc: 2 * NCH + cc + 1]),
                        reads=[bp, bg("bvec")], writes=[bg("hbuf")])
            for g in range(4):
                slot = g % 2
                wsl = wpj_c[slot]
                bw = bg(f"wpj{slot}")
                em.dma(f"lwpj{slot}", lambda e, d=wsl, gg=g: e.dma_start(
                    out=d.rearrange("p (c n) -> p c n", c=MLP // 128, n=256),
                    in_=s_pjT.rearrange("(c p) n -> p c n", c=MLP // 128, p=128)[:, :, gg * 256:(gg + 1) * 256]),
                    reads=[bg("s_pjT")], writes=[bw])
                for oc in range(2):
                    co = g * 2 + oc
                    pidx = (0, 1, 4, 5)[co % 4]
                    php = psum[pidx][:, 0:TB]
                    bp = bg(f"ps{pidx}")
                    for c in range(MLP // 128):
                        lh = wsl[:, c * 256 + oc * 128: c * 256 + (oc + 1) * 128]
                        rh = hbuf_c[:, c * TB:(c + 1) * TB]
                        em.op("pe", MM(php, lh, rh, c == 0, c == MLP // 128 - 1),
                              reads=[bw, bg("hbuf")], writes=[bp], last=(c == MLP // 128 - 1))
                    zc = zres[:, co * TB:(co + 1) * TB]
                    em.op("dve", lambda e, a=zc, p=php, cc=co: e.scalar_tensor_tensor(
                        out=a, in0=p, scalar=bvec[:, NCH + cc: NCH + cc + 1], in1=a,
                        op0=ALU.add, op1=ALU.add),
                        reads=[bp, bg("bvec"), bg("zres")], writes=[bg("zres")])
            b7, b6 = bg("ps7"), bg("ps6")
            for c in range(NCH):
                ss = c % 2
                xb = sqb[:, ss * TB:(ss + 1) * TB]
                zc = zres[:, c * TB:(c + 1) * TB]
                z0c = z0[:, c * TB:(c + 1) * TB]
                em.op("dve", lambda e, d=xb, a=zc, b=z0c: e.tensor_tensor(
                    out=d, in0=a, in1=b, op=ALU.add),
                    reads=[bg("zres"), bg("z0")], writes=[bg(f"sqb{ss}")])
                sq = sqb[:, (2 + ss) * TB:(3 + ss) * TB]
                em.op("dve", lambda e, d=sq, s=xb: e.tensor_tensor(
                    out=d, in0=s, in1=s, op=ALU.mult),
                    reads=[bg(f"sqb{ss}")], writes=[bg(f"sqb{2 + ss}")])
                em.op("pe", MM(psum[7][0:1, 0:TB], onec_b[:], xb, c == 0, c == NCH - 1),
                      reads=[bg("onec_b"), bg(f"sqb{ss}")], writes=[b7], last=True)
                em.op("pe", MM(psum[6][0:1, 0:TB], onec_b[:], sq, c == 0, c == NCH - 1),
                      reads=[bg("onec_b"), bg(f"sqb{2 + ss}")], writes=[b6], last=True)
            rf, rb = bg("rows_f"), bg("rows_b")
            m_r, e2_r, s_r = rows_f[:, 0:TB], rows_f[:, TB:2 * TB], rows_f[:, 2 * TB:3 * TB]
            em.op("dve", lambda e: e.tensor_scalar(out=m_r, in0=psum[7][0:1, 0:TB], scalar1=INV_DM,
                                                   scalar2=None, op0=ALU.mult), reads=[b7], writes=[rf])
            em.op("dve", lambda e: e.tensor_scalar(out=e2_r, in0=psum[6][0:1, 0:TB], scalar1=INV_DM,
                                                   scalar2=None, op0=ALU.mult), reads=[b6], writes=[rf])
            em.op("dve", lambda e: e.tensor_tensor(out=s_r, in0=m_r, in1=m_r, op=ALU.mult),
                  reads=[rf], writes=[rf])
            em.op("dve", lambda e: e.tensor_tensor(out=s_r, in0=e2_r, in1=s_r, op=ALU.subtract),
                  reads=[rf], writes=[rf])
            em.op("act", lambda e: e.activation(out=s_r, in_=s_r, func=AF.Sqrt, bias=epsr[:, 0:1]),
                  reads=[rf, bg("epsr")], writes=[rf])
            em.op("dve", lambda e: e.reciprocal(out=s_r, in_=s_r), reads=[rf], writes=[rf])
            rp = {}
            for (snm, bo, row) in (("m", 0, m_r), ("s", 2, s_r)):
                hi = rows_b[:, bo * TB:(bo + 1) * TB]
                lo = rows_b[:, (bo + 1) * TB:(bo + 2) * TB]
                em.op("dve", lambda e, r=row, h_=hi: e.tensor_copy(out=h_, in_=r),
                      reads=[rf], writes=[rb])
                em.op("dve", lambda e, r=row, h_=hi, l_=lo: e.tensor_tensor(
                    out=l_, in0=r, in1=h_, op=ALU.subtract),
                    reads=[rf, rb], writes=[rb])
                rp[snm] = (hi, lo)
            bm3 = bcast(rp["m"], 4)
            bs3_ = bcast(rp["s"], 5)
            for c in range(NCH):
                zc = zres[:, c * TB:(c + 1) * TB]
                z0c = z0[:, c * TB:(c + 1) * TB]
                yc = y_c[:, c * TB:(c + 1) * TB]
                em.op("dve", lambda e, a=tscr[:, 0:TB], b=zc, c0=z0c: e.tensor_tensor(
                    out=a, in0=b, in1=c0, op=ALU.add),
                    reads=[bg("zres"), bg("z0")], writes=[bg("tscr")])
                em.op("dve", lambda e, a=tscr[:, 0:TB]: e.tensor_tensor(
                    out=a, in0=a, in1=psum[4][:, 0:TB], op=ALU.subtract),
                    reads=[bg("tscr"), bm3], writes=[bg("tscr")])
                em.op("dve", lambda e, a=yc, b=tscr[:, 0:TB]: e.tensor_tensor(
                    out=a, in0=b, in1=psum[5][:, 0:TB], op=ALU.mult),
                    reads=[bg("tscr"), bs3_], writes=[bg("y")])
            for j in range(ntiles):
                bp = bg("ps2")
                for c in range(NCH):
                    lh = y_c[:, c * TB + j * 128: c * TB + (j + 1) * 128]
                    rh = wrt[:, c * NE:(c + 1) * NE]
                    em.op("pe", MM(psum[2][:, 0:NE], lh, rh, c == 0, c == NCH - 1),
                          reads=[bg("y"), bg("wrt")], writes=[bp], last=(c == NCH - 1))
                bc, bcm = bg("csc"), bg("comb")
                cj = comb[:, j * NE:(j + 1) * NE]
                em.op("dve", lambda e: e.tensor_reduce(out=csc[:, 8:9], in_=psum[2][:, 0:NE],
                                                       axis=AX.X, op=ALU.max, negate=True),
                      reads=[bp], writes=[bc])
                em.op("act", lambda e, d=cj: e.activation(out=d, in_=psum[2][:, 0:NE], func=AF.Exp,
                                                          bias=csc[:, 8:9], accum_out=csc[:, 9:10]),
                      reads=[bp, bc], writes=[bcm, bc])
                em.op("dve", lambda e: e.reciprocal(out=csc[:, 10:11], in_=csc[:, 9:10]),
                      reads=[bc], writes=[bc])
                em.op("dve", lambda e, d=cj: e.tensor_scalar(
                    out=csc[:, 0:8], in0=d, scalar1=1.0, scalar2=None, op0=ALU.is_equal),
                    reads=[bcm], writes=[bc])
                em.op("dve", lambda e, d=cj: e.scalar_tensor_tensor(
                    out=csc[:, 0:8], in0=csc[:, 0:8], scalar=-3.0, in1=d, op0=ALU.mult, op1=ALU.add),
                    reads=[bc, bcm], writes=[bc])
                em.op("dve", lambda e: e.tensor_reduce(out=csc[:, 12:13], in_=csc[:, 0:8],
                                                       axis=AX.X, op=ALU.max),
                      reads=[bc], writes=[bc])
                em.op("dve", lambda e, d=cj: e.tensor_scalar(
                    out=csc[:, 0:8], in0=d, scalar1=csc[:, 12:13], scalar2=None, op0=ALU.is_ge),
                    reads=[bcm, bc], writes=[bc])
                em.op("dve", lambda e, d=cj: e.scalar_tensor_tensor(
                    out=d, in0=d, scalar=csc[:, 10:11], in1=csc[:, 0:8], op0=ALU.mult, op1=ALU.mult),
                    reads=[bc, bcm], writes=[bcm])

        tail_units = []
        for blk in range(nblocks):
            em.barrier(G_G + G_E, G_CD)
            st1 = emit_A(blk)
            emit_Bp(blk, st1)
            head_units = []
            npairs = (ntiles + 1) // 2
            for pr in range(npairs):
                head_units.append(make_C_pair(blk, pr))
                for j in (2 * pr, 2 * pr + 1):
                    if j < ntiles:
                        head_units.extend(make_D_units(blk, j, pr))
            nj = min(48, len(tail_units))
            ti = 0
            nh_ = len(head_units)
            for i, hu in enumerate(head_units):
                hu()
                take = ((i + 1) * nj) // nh_ - ti
                for _ in range(take):
                    tail_units[ti]()
                    ti += 1
            while ti < nj:
                tail_units[ti]()
                ti += 1
            lat_chunks = [make_lat_chunk(blk, c) for c in range(NCH)]
            li = 0
            while ti < len(tail_units):
                if li < NCH:
                    lat_chunks[li]()
                    li += 1
                tail_units[ti]()
                ti += 1
            while li < NCH:
                lat_chunks[li]()
                li += 1
            emit_serial_tailstages(blk)
            tail_units = build_tail(blk)
        em.barrier(G_G + G_E, G_CD)
        for u in tail_units:
            u()

        st_out = em.stream("sout")
        em.sp_wait((st_out.sem, st_out.cnt))

        with nc.Block() as block:
            @block.sync
            def _(sync):
                for f in em.sp_q:
                    f(sync)

            @block.vector
            def _(vector):
                for f in em.engs["dve"].q:
                    f(vector)

            @block.scalar
            def _(scalar):
                for f in em.engs["act"].q:
                    f(scalar)

            @block.tensor
            def _(tensor):
                for f in em.engs["pe"].q:
                    f(tensor)

            @block.gpsimd
            def _(gpsimd):
                for f in em.engs["pool"].q:
                    f(gpsimd)

    return nc


def kernel(**inputs):
    x = np.ascontiguousarray(np.asarray(inputs["L0_attn_x"], dtype=np.float32))
    BSH = x.shape
    xf = x.reshape(-1, DM)
    TC = xf.shape[0] // NCORES

    def arr(name):
        return np.ascontiguousarray(np.asarray(inputs[name], dtype=np.float32))

    row = lambda a: np.ascontiguousarray(np.asarray(a, dtype=np.float32).reshape(1, -1))
    common = {
        "qw": arr("L0_attn_q_w"), "kw": arr("L0_attn_k_w"), "vw": arr("L0_attn_v_w"),
        "ow": arr("L0_attn_o_w"), "fcw": arr("L0_attn_fc_w"), "pjw": arr("L0_attn_proj_w"),
        "qb": row(inputs["L0_attn_q_b"]), "kb": row(inputs["L0_attn_k_b"]),
        "vb": row(inputs["L0_attn_v_b"]), "ob": row(inputs["L0_attn_o_b"]),
        "fcb": row(inputs["L0_attn_fc_b"]), "pjb": row(inputs["L0_attn_proj_b"]),
        "l1w": row(inputs["L0_attn_ln1_w"]), "l1b": row(inputs["L0_attn_ln1_b"]),
        "l2w": row(inputs["L0_attn_ln2_w"]), "l2b": row(inputs["L0_attn_ln2_b"]),
        "rtw": arr("L0_ffn_router_w"),
        "gw": arr("L0_ffn_gate_w"), "uw": arr("L0_ffn_up_w"), "dw": arr("L0_ffn_down_w"),
        "sgw": arr("L0_ffn_shared_gate_w"), "suw": arr("L0_ffn_shared_up_w"),
        "sdw": arr("L0_ffn_shared_down_w"),
        "eye_in": np.eye(128, dtype=np.float32),
    }
    nc = build_nc(TC)
    in_maps = [dict(common, x=np.ascontiguousarray(xf[c * TC:(c + 1) * TC])) for c in range(NCORES)]
    res = run_bass_kernel_spmd(nc, in_maps, core_ids=list(range(NCORES)))
    out = np.concatenate([r["out"] for r in res.results], axis=0)
    return out.reshape(BSH)


# version 14 (final)
# speedup vs baseline: 1.1066x; 1.1066x over previous
import contextlib
import numpy as np
import ml_dtypes
import concourse.bass as bass
import concourse.mybir as mybir
from concourse.bass_utils import run_bass_kernel_spmd

F32 = mybir.dt.float32
BF16 = mybir.dt.bfloat16
AF = mybir.ActivationFunctionType
ALU = mybir.AluOpType
AX = mybir.AxisListType

NCORES = 8
DM = 1024
NCH = DM // 128
H, D, R = 16, 64, 16
HQ = 4
MLP = 4096
NE = 8
FS = 2048
EPS = 1e-5
INV_DM = 1.0 / DM


def mkap(ap, dims, off=0):
    APc = type(ap)
    return APc(ap.tensor, ap.offset + off, [list(ap.ap[0])] + [list(d) for d in dims])


class Buf:
    __slots__ = ("name", "wev", "revs")

    def __init__(self, name):
        self.name = name
        self.wev = []
        self.revs = []


class EngState:
    __slots__ = ("key", "sem", "cnt", "last_wait", "q", "drained_at")

    def __init__(self, key, sem):
        self.key = key
        self.sem = sem
        self.cnt = 0
        self.last_wait = {}
        self.q = []
        self.drained_at = 0


class Stream:
    __slots__ = ("name", "sem", "cnt")

    def __init__(self, name, sem):
        self.name = name
        self.sem = sem
        self.cnt = 0


class Emitter:
    def __init__(self, nc, es):
        self.nc = nc
        self.es = es
        self.engs = {}
        for key in ("pe", "dve", "act", "pool"):
            sem = es.enter_context(nc.semaphore(f"sem_{key}"))
            self.engs[key] = EngState(key, sem)
        self.sp_q = []
        self.sp_last_wait = {}
        self.streams = {}
        self.pe_pend_r = []
        self.pe_pend_w = []

    def stream(self, name):
        if name not in self.streams:
            sem = self.es.enter_context(self.nc.semaphore(f"s_{name}"))
            self.streams[name] = Stream(name, sem)
        return self.streams[name]

    def op(self, key, fn, reads=(), writes=(), last=True):
        eng = self.engs[key]
        if key != "pe":
            for b in writes:
                assert b not in self.pe_pend_r and b not in self.pe_pend_w, \
                    f"write to {b.name} while a PE group holds a pending ref"
        deps = []
        for b in reads:
            deps.extend(b.wev)
        for b in writes:
            deps.extend(b.wev)
            deps.extend(b.revs)
        need_drain = False
        cross = {}
        for (sem, val) in deps:
            if sem is eng.sem:
                if val > eng.drained_at:
                    need_drain = True
            else:
                kk = id(sem)
                if cross.get(kk, (None, 0))[1] < val:
                    cross[kk] = (sem, val)
        for (sem, val) in cross.values():
            if eng.last_wait.get(id(sem), 0) < val:
                eng.q.append(lambda e, s=sem, v=val: e.wait_ge(s, v))
                eng.last_wait[id(sem)] = val
        if need_drain and key != "pe":
            eng.q.append(lambda e: e.drain())
            eng.drained_at = eng.cnt
        if last:
            eng.cnt += 1
            myev = (eng.sem, eng.cnt)
            eng.q.append(lambda e, f=fn, s=eng.sem: f(e).then_inc(s, 1))
            wl = list(writes) + (self.pe_pend_w if key == "pe" else [])
            rl = list(reads) + (self.pe_pend_r if key == "pe" else [])
            for b in wl:
                b.wev = [myev]
                b.revs = []
            for b in rl:
                b.revs.append(myev)
            if key == "pe":
                self.pe_pend_r = []
                self.pe_pend_w = []
            return myev
        else:
            assert key == "pe"
            eng.q.append(lambda e, f=fn: f(e))
            self.pe_pend_r.extend(reads)
            self.pe_pend_w.extend(writes)
            return None

    def dma(self, stream_name, fn, reads=(), writes=(), war=(), queue="sp"):
        st = self.stream(stream_name)
        for b in writes:
            assert b not in self.pe_pend_r and b not in self.pe_pend_w, \
                f"dma write to {b.name} while a PE group holds a pending ref"
        deps = []
        for b in reads:
            deps.extend(b.wev)
        for b in list(writes) + list(war):
            deps.extend(b.wev)
            deps.extend(b.revs)
        cross = {}
        for (sem, val) in deps:
            kk = id(sem)
            if cross.get(kk, (None, 0))[1] < val:
                cross[kk] = (sem, val)
        if queue == "sp":
            for (sem, val) in cross.values():
                if self.sp_last_wait.get(id(sem), 0) < val:
                    self.sp_q.append(lambda e, s=sem, v=val: e.wait_ge(s, v))
                    self.sp_last_wait[id(sem)] = val
            st.cnt += 16
            ev = (st.sem, st.cnt)
            self.sp_q.append(lambda e, f=fn, s=st.sem: f(e).then_inc(s, 16))
        else:
            eng = self.engs["pool" if queue == "gp" else "act"]
            for (sem, val) in cross.values():
                if eng.last_wait.get(id(sem), 0) < val:
                    eng.q.append(lambda e, s=sem, v=val: e.wait_ge(s, v))
                    eng.last_wait[id(sem)] = val
            st.cnt += 16
            ev = (st.sem, st.cnt)
            eng.q.append(lambda e, f=fn, s=st.sem: f(e).then_inc(s, 16))
        for b in writes:
            b.wev = [ev]
            b.revs = []
        for b in reads:
            b.revs.append(ev)
        return ev

    def seal(self, stream_names, bufs):
        if isinstance(stream_names, str):
            stream_names = [stream_names]
        evs = [(self.stream(n).sem, self.stream(n).cnt) for n in stream_names]
        for b in bufs:
            b.wev = list(evs)
            b.revs = []

    def barrier(self, old_bufs, new_bufs):
        evs = []
        for b in old_bufs:
            evs.extend(b.wev)
            evs.extend(b.revs)
            b.wev = []
            b.revs = []
        for b in new_bufs:
            b.revs.extend(evs)

    def sp_wait(self, ev):
        sem, val = ev
        if self.sp_last_wait.get(id(sem), 0) < val:
            self.sp_q.append(lambda e, s=sem, v=val: e.wait_ge(s, v))
            self.sp_last_wait[id(sem)] = val


def build_nc(TC):
    TB = min(512, TC)
    nblocks = TC // TB
    ntiles = TB // 128
    assert TC % TB == 0 and TB % 128 == 0

    nc = bass.Bass("TRN2", target_bir_lowering=False)

    def din(name, shape):
        return nc.dram_tensor(name, shape, F32, kind="ExternalInput").ap()

    x_in = din("x", [TC, DM])
    qw, kw, vw, ow = din("qw", [DM, DM]), din("kw", [DM, DM]), din("vw", [DM, DM]), din("ow", [DM, DM])
    fcw, pjw = din("fcw", [MLP, DM]), din("pjw", [DM, MLP])
    qb_i, kb_i, vb_i = din("qb", [1, DM]), din("kb", [1, DM]), din("vb", [1, DM])
    ob_i, fcb_i, pjb_i = din("ob", [1, DM]), din("fcb", [1, MLP]), din("pjb", [1, DM])
    l1w_i, l1b_i = din("l1w", [1, DM]), din("l1b", [1, DM])
    l2w_i, l2b_i = din("l2w", [1, DM]), din("l2b", [1, DM])
    rtw = din("rtw", [DM, NE])
    gw, uw, dwt = din("gw", [NE, DM, DM]), din("uw", [NE, DM, DM]), din("dw", [NE, DM, DM])
    sgw, suw, sdw = din("sgw", [DM, FS]), din("suw", [DM, FS]), din("sdw", [FS, DM])
    eye_i = din("eye_in", [128, 128])
    out_t = nc.dram_tensor("out", [TC, DM], F32, kind="ExternalOutput").ap()

    def dscr(name, shape):
        return nc.dram_tensor(name, shape, BF16, kind="Internal").ap()

    s_x0 = dscr("s_x0", [TC, DM])
    s_q, s_k, s_v, s_o = dscr("s_q", [DM, DM]), dscr("s_k", [DM, DM]), dscr("s_v", [DM, DM]), dscr("s_o", [DM, DM])
    s_fc, s_pj = dscr("s_fc", [MLP, DM]), dscr("s_pj", [DM, MLP])
    s_qT, s_kT = dscr("s_qT", [DM, DM]), dscr("s_kT", [DM, DM])
    s_vT, s_oT = dscr("s_vT", [DM, DM]), dscr("s_oT", [DM, DM])
    s_fcT, s_pjT = dscr("s_fcT", [DM, MLP]), dscr("s_pjT", [MLP, DM])
    s_g, s_u, s_d = dscr("s_g", [NE, DM, DM]), dscr("s_u", [NE, DM, DM]), dscr("s_d", [NE, DM, DM])
    s_sg, s_su, s_sd = dscr("s_sg", [DM, FS]), dscr("s_su", [DM, FS]), dscr("s_sd", [FS, DM])
    s_rt = dscr("s_rt", [DM, NE])
    s_at = dscr("s_at", [TC, DM])

    es = contextlib.ExitStack()
    with es:
        def sb(name, p, f, dt):
            return es.enter_context(nc.sbuf_tensor(name, [p, f], dt))

        z0 = sb("z0", 128, NCH * TB, BF16)
        zres = sb("zres", 128, NCH * TB, F32)
        xn = sb("xn", 128, NCH * TB, BF16)
        routed = sb("routed", 128, ntiles * DM, F32)
        outsb = sb("outsb", 128, DM, F32)
        x0re = sb("x0re", 128, DM, F32)
        sqb = sb("sqb", 128, 4 * TB, BF16)
        elus = sb("elus", 128, 1024, F32)
        tscr = sb("tscr", 128, TB, F32)
        atm = sb("atm", 128, 2 * DM, BF16)
        abuf = sb("abuf", 128, R * H, F32)
        abb = sb("abb", 128, R * H, BF16)
        avb = sb("avb", 128, H * D, F32)
        comb = sb("comb", 128, ntiles * NE, F32)
        csc = sb("csc", 128, 16, F32)
        eye = sb("eye", 128, 128, F32)
        ones_b = sb("ones_b", 1, TB, BF16)
        onec_b = sb("onec_b", 128, 1, BF16)
        epsr = sb("epsr", 1, 1, F32)
        lnv = sb("lnv", 128, 4 * NCH, F32)
        bvec = sb("bvec", 128, 2 * NCH + MLP // 128, F32)
        qkvb = sb("qkvb", 1, 3 * DM, BF16)
        wrt = sb("wrt", 128, NCH * NE, BF16)
        rows_f = sb("rows_f", 1, 4 * TB, F32)
        rows_b = sb("rows_b", 1, 6 * TB, BF16)
        UN = 45056
        una = sb("una", 128, UN, BF16)

        CARVE = {
            "mid": (0, 4096), "mids": (4096, 2048),
            "pool0": (6144, 4096), "pool1": (10240, 4096), "pool2": (14336, 4096),
            "qtm": (18432, 2048), "kdup": (20480, 4096), "vdup": (24576, 4096),
            "wqvA": (28672, 4096), "wqvB": (32768, 4096), "p3b": (36864, 4096),
            "y": (40960, 4096),
            "attnf": (0, 4096),
            "hbuf": (0, 16384), "wfc0": (16384, 4096), "wfc1": (20480, 4096),
            "wpj0": (24576, 8192), "wpj1": (32768, 8192),
        }

        def carve(nm):
            o, s = CARVE[nm]
            return una[:, o:o + s]

        psum = [es.enter_context(nc.psum_tensor(f"ps{i}", [128, 512], F32)) for i in range(8)]

        em = Emitter(nc, es)
        Bufs = {}

        def bg(name):
            if name not in Bufs:
                Bufs[name] = Buf(name)
            return Bufs[name]

        G_CD = [bg(n) for n in ("mid", "mids", "pool0", "pool1", "pool2", "qtm",
                                 "kdup", "vdup", "wqvA", "wqvB", "p3b")]
        G_E = [bg("attnf")]
        G_G = [bg(n) for n in ("hbuf", "wfc0", "wfc1", "wpj0", "wpj1")]

        em.op("dve", lambda e: e.memset(ones_b[:], 1.0), writes=[bg("ones_b")])
        em.op("dve", lambda e: e.memset(onec_b[:], 1.0), writes=[bg("onec_b")])
        em.op("dve", lambda e: e.memset(epsr[:], EPS), writes=[bg("epsr")])

        em.dma("cx", lambda e: e.dma_start(out=s_x0[:, :], in_=x_in[:, :]), queue="gp")
        em.seal("cx", [bg("s_x0")])
        em.dma("crt", lambda e: e.dma_start(out=s_rt[:, :], in_=rtw[:, :]), queue="gp")
        em.seal("crt", [bg("s_rt")])
        for i, src in enumerate((qb_i, kb_i, vb_i)):
            em.dma("cqb", lambda e, s=src, o=i * DM: e.dma_start(out=qkvb[:, o:o + DM], in_=s[:, :]),
                   queue="gp")
        em.seal("cqb", [bg("qkvb")])
        for nm, src, dst in (("s_q", qw, s_q), ("s_k", kw, s_k), ("s_v", vw, s_v),
                             ("s_o", ow, s_o), ("s_fc", fcw, s_fc), ("s_pj", pjw, s_pj)):
            em.dma("c" + nm, lambda e, s=src, d=dst: e.dma_start(out=d[:, :], in_=s[:, :]), queue="gp")
            em.seal("c" + nm, [bg(nm)])
        for ei in range(NE):
            em.dma(f"cmoe{ei}", lambda e, i=ei: e.dma_start(out=s_g[i, :, :], in_=gw[i, :, :]), queue="gp")
            em.dma(f"cmoe{ei}", lambda e, i=ei: e.dma_start(out=s_u[i, :, :], in_=uw[i, :, :]), queue="gp")
            em.dma(f"cmoe{ei}", lambda e, i=ei: e.dma_start(out=s_d[i, :, :], in_=dwt[i, :, :]), queue="gp")
            em.seal(f"cmoe{ei}", [bg(f"s_g{ei}"), bg(f"s_u{ei}"), bg(f"s_d{ei}")])
        em.dma("cmsh", lambda e: e.dma_start(out=s_sg[:, :], in_=sgw[:, :]), queue="gp")
        em.dma("cmsh", lambda e: e.dma_start(out=s_su[:, :], in_=suw[:, :]), queue="gp")
        em.dma("cmsh", lambda e: e.dma_start(out=s_sd[:, :], in_=sdw[:, :]), queue="gp")
        em.seal("cmsh", [bg("s_sg"), bg("s_su"), bg("s_sd")])

        for vi, src in enumerate((l1w_i, l1b_i, l2w_i, l2b_i)):
            for c in range(NCH):
                em.dma("slnv", lambda e, s=src, v=vi, cc=c: e.dma_start(
                    out=lnv[:, v * NCH + cc: v * NCH + cc + 1],
                    in_=s[0, cc * 128:(cc + 1) * 128].unsqueeze(1)))
        em.seal("slnv", [bg("lnv")])
        for base, src, n in ((0, ob_i, NCH), (NCH, pjb_i, NCH), (2 * NCH, fcb_i, MLP // 128)):
            for c in range(n):
                em.dma("sbv", lambda e, s=src, b=base, cc=c: e.dma_start(
                    out=bvec[:, b + cc: b + cc + 1],
                    in_=s[0, cc * 128:(cc + 1) * 128].unsqueeze(1)))
        em.seal("sbv", [bg("bvec")])

        p3b_t = carve("p3b")
        stg_sp = (carve("p3b"), carve("kdup"), carve("vdup"))
        stg_act = (carve("pool0"), carve("pool1"), carve("pool2"))

        def prep_T(nm, s_src, s_dst, owid, ichunks, stgs, slot_names, queue):
            for ic in range(ichunks):
                sl = ic % 3
                stg = stgs[sl][:, 0:owid]
                sbuf_b = bg(slot_names[sl])
                em.dma(f"tp_{queue}{sl}", lambda e, s=s_src, i=ic, g=stg, ow=owid: e.dma_start(
                    out=g, in_=s[0:ow, i * 128:(i + 1) * 128], transpose=True),
                    reads=[bg(nm)], writes=[sbuf_b], queue=queue)
                em.dma(f"tw{nm}", lambda e, d=s_dst, i=ic, g=stg, ow=owid: e.dma_start(
                    out=d[i * 128:(i + 1) * 128, 0:ow], in_=g),
                    reads=[sbuf_b], writes=[bg(nm + "T")], queue=queue)
            em.seal(f"tw{nm}", [bg(nm + "T")])

        SP_SLOTS = ("p3b", "kdup", "vdup")
        POOL_SLOTS = ("pool0", "pool1", "pool2")
        prep_T("s_q", s_q, s_qT, DM, NCH, stg_sp, SP_SLOTS, "sp")
        prep_T("s_k", s_k, s_kT, DM, NCH, stg_sp, SP_SLOTS, "sp")
        prep_T("s_v", s_v, s_vT, DM, NCH, stg_sp, SP_SLOTS, "sp")

        def deferred_prep():
            prep_T("s_o", s_o, s_oT, DM, NCH, stg_act, POOL_SLOTS, "sp")
            prep_T("s_fc", s_fc, s_fcT, MLP, NCH, stg_act, POOL_SLOTS, "sp")
            prep_T("s_pj", s_pj, s_pjT, DM, MLP // 128, stg_act, POOL_SLOTS, "sp")
            for c in range(NCH):
                em.dma("lrt", lambda e, cc=c: e.dma_start(
                    out=wrt[:, cc * NE:(cc + 1) * NE], in_=s_rt[cc * 128:(cc + 1) * 128, :]),
                    reads=[bg("s_rt")])
            em.seal("lrt", [bg("wrt")])

        em.dma("seye", lambda e: e.dma_start(out=eye[:], in_=eye_i[:, :]))
        em.seal("seye", [bg("eye")])


        def MM(out_ap, lh_ap, rh_ap, st_, sp_):
            return lambda e: e.matmul(out=out_ap, lhsT=lh_ap, rhs=rh_ap, start=st_, stop=sp_)

        def ln_stats(src_sb, src_buf, src_is_f32, want_s2):
            b7, b6 = bg("ps7"), bg("ps6")
            for c in range(NCH):
                sl = src_sb[:, c * TB:(c + 1) * TB]
                if src_is_f32:
                    ss = c % 2
                    xb = sqb[:, ss * TB:(ss + 1) * TB]
                    em.op("dve", lambda e, s=sl, d=xb: e.tensor_copy(out=d, in_=s),
                          reads=[src_buf], writes=[bg(f"sqb{ss}")])
                    xb_b = bg(f"sqb{ss}")
                else:
                    xb = sl
                    xb_b = src_buf
                em.op("pe", MM(psum[7][0:1, 0:TB], onec_b[:], xb, c == 0, c == NCH - 1),
                      reads=[bg("onec_b"), xb_b], writes=[b7], last=True)
            for c in range(NCH):
                if src_is_f32:
                    sl = src_sb[:, c * TB:(c + 1) * TB]
                    ss = 2 + c % 2
                    sq = sqb[:, ss * TB:(ss + 1) * TB]
                    em.op("act", lambda e, s=sl, d=sq: e.activation(out=d, in_=s, func=AF.Square),
                          reads=[src_buf], writes=[bg(f"sqb{ss}")])
                    sq_b = bg(f"sqb{ss}")
                else:
                    sl = src_sb[:, c * TB:(c + 1) * TB]
                    ss = 2 + c % 2
                    sq = sqb[:, ss * TB:(ss + 1) * TB]
                    em.op("dve", lambda e, s=sl, d=sq: e.tensor_tensor(out=d, in0=s, in1=s, op=ALU.mult),
                          reads=[src_buf], writes=[bg(f"sqb{ss}")])
                    sq_b = bg(f"sqb{ss}")
                em.op("pe", MM(psum[6][0:1, 0:TB], onec_b[:], sq, c == 0, c == NCH - 1),
                      reads=[bg("onec_b"), sq_b], writes=[b6], last=True)
            rf, rb = bg("rows_f"), bg("rows_b")
            m_r = rows_f[:, 0:TB]
            e2_r = rows_f[:, TB:2 * TB]
            s_r = rows_f[:, 2 * TB:3 * TB]
            s2_r = rows_f[:, 3 * TB:4 * TB]
            em.op("dve", lambda e: e.tensor_scalar(out=m_r, in0=psum[7][0:1, 0:TB], scalar1=INV_DM,
                                                   scalar2=None, op0=ALU.mult),
                  reads=[b7], writes=[rf])
            em.op("dve", lambda e: e.tensor_scalar(out=e2_r, in0=psum[6][0:1, 0:TB], scalar1=INV_DM,
                                                   scalar2=None, op0=ALU.mult),
                  reads=[b6], writes=[rf])
            em.op("dve", lambda e: e.tensor_tensor(out=s_r, in0=m_r, in1=m_r, op=ALU.mult),
                  reads=[rf], writes=[rf])
            em.op("dve", lambda e: e.tensor_tensor(out=s_r, in0=e2_r, in1=s_r, op=ALU.subtract),
                  reads=[rf], writes=[rf])
            em.op("act", lambda e: e.activation(out=s_r, in_=s_r, func=AF.Sqrt, bias=epsr[:, 0:1]),
                  reads=[rf, bg("epsr")], writes=[rf])
            em.op("dve", lambda e: e.reciprocal(out=s_r, in_=s_r), reads=[rf], writes=[rf])
            if want_s2:
                em.op("dve", lambda e: e.tensor_tensor(out=s2_r, in0=s_r, in1=s_r, op=ALU.mult),
                      reads=[rf], writes=[rf])
                em.op("dve", lambda e: e.tensor_scalar(out=s2_r, in0=s2_r, scalar1=-EPS,
                                                       scalar2=1.0 + EPS, op0=ALU.mult, op1=ALU.add),
                      reads=[rf], writes=[rf])
                em.op("act", lambda e: e.activation(out=s2_r, in_=s2_r, func=AF.Sqrt),
                      reads=[rf], writes=[rf])
                em.op("dve", lambda e: e.reciprocal(out=s2_r, in_=s2_r), reads=[rf], writes=[rf])
                em.op("dve", lambda e: e.tensor_tensor(out=s2_r, in0=s2_r, in1=s_r, op=ALU.mult),
                      reads=[rf], writes=[rf])
            outs = {}
            stats = [("m", 0, m_r), ("s", 2, s_r)] + ([("s3", 4, s2_r)] if want_s2 else [])
            for (snm, bo, row) in stats:
                hi = rows_b[:, bo * TB:(bo + 1) * TB]
                lo = rows_b[:, (bo + 1) * TB:(bo + 2) * TB]
                em.op("dve", lambda e, r=row, h_=hi: e.tensor_copy(out=h_, in_=r),
                      reads=[rf], writes=[rb])
                em.op("dve", lambda e, r=row, h_=hi, l_=lo: e.tensor_tensor(
                    out=l_, in0=r, in1=h_, op=ALU.subtract),
                    reads=[rf, rb], writes=[rb])
                outs[snm] = (hi, lo)
            return outs

        def bcast(rowpair, pidx):
            hi, lo = rowpair
            bp = bg(f"ps{pidx}")
            em.op("pe", MM(psum[pidx][:, 0:TB], ones_b[0:1, 0:128], hi, True, False),
                  reads=[bg("ones_b"), bg("rows_b")], writes=[bp], last=False)
            em.op("pe", MM(psum[pidx][:, 0:TB], ones_b[0:1, 0:128], lo, False, True),
                  reads=[bg("ones_b"), bg("rows_b")], writes=[bp], last=True)
            return bp

        bias_roff = {"q": 0, "k": DM, "v": 2 * DM}
        qtm_c, kdup_c, vdup_c = carve("qtm"), carve("kdup"), carve("vdup")
        attnf_c = carve("attnf")
        wqv_c = [carve("wqvA"), carve("wqvB")]
        hbuf_c = carve("hbuf")
        wfc_c = [carve("wfc0"), carve("wfc1")]
        wpj_c = [carve("wpj0"), carve("wpj1")]
        mid_c, mids_c = carve("mid"), carve("mids")
        pool_c = [carve("pool0"), carve("pool1"), carve("pool2")]
        y_c = carve("y")
        pool_seq = [0]
        wqv_seq = [0]

        def wqv_load(s_wT, src_buf_name, half):
            sl = wqv_seq[0] % 2
            wqv_seq[0] += 1
            wsl = wqv_c[sl]
            bw = bg("wqv" + "AB"[sl])
            em.dma(f"lwqv{sl}", lambda e, d=wsl, s=s_wT, o=half * 512: e.dma_start(
                out=d.rearrange("p (c n) -> p c n", c=NCH, n=512),
                in_=s.rearrange("(c p) n -> p c n", c=NCH, p=128)[:, :, o:o + 512]),
                reads=[bg(src_buf_name)], writes=[bw])
            return wsl, bw

        def emit_A(blk):
            t0 = blk * TB
            bz0 = bg("z0")
            for c in range(NCH):
                em.dma("lz0", lambda e, cc=c, tt=t0: e.dma_start(
                    out=z0[:, cc * TB:(cc + 1) * TB],
                    in_=s_x0[tt:tt + TB, cc * 128:(cc + 1) * 128], transpose=True),
                    reads=[bg("s_x0")], war=([bz0] if c == 0 else []))
            em.seal("lz0", [bz0])
            return ln_stats(z0, bz0, False, True)

        def emit_Bp(blk, st1):
            bm = bcast(st1["m"], 4)
            bs3 = bcast(st1["s3"], 6)
            bz0, bxn = bg("z0"), bg("xn")
            for c in range(NCH):
                z0c = z0[:, c * TB:(c + 1) * TB]
                xnc = xn[:, c * TB:(c + 1) * TB]
                t_ = tscr[:, 0:TB]
                em.op("dve", lambda e, a=t_, b=z0c: e.tensor_tensor(
                    out=a, in0=b, in1=psum[4][:, 0:TB], op=ALU.subtract),
                    reads=[bz0, bm], writes=[bg("tscr")])
                em.op("dve", lambda e, a=xnc, b=t_, cc=c: e.scalar_tensor_tensor(
                    out=a, in0=b, scalar=lnv[:, cc:cc + 1], in1=psum[6][:, 0:TB],
                    op0=ALU.mult, op1=ALU.mult),
                    reads=[bg("tscr"), bs3, bg("lnv")], writes=[bxn])
                em.op("dve", lambda e, a=xnc, cc=c: e.tensor_scalar(
                    out=a, in0=a, scalar1=lnv[:, NCH + cc:NCH + cc + 1], scalar2=None,
                    op0=ALU.add),
                    reads=[bg("lnv")], writes=[bxn])

        def make_C_pair(blk, pr):
            def unit():
                jlist = [j for j in (2 * pr, 2 * pr + 1) if j < ntiles]
                bxn = bg("xn")
                for pname, s_wT in (("q", s_qT), ("k", s_kT), ("v", s_vT)):
                    for half in range(2):
                        wsl, bw = wqv_load(s_wT, "s_" + pname + "T", half)
                        o0 = half * 512
                        for j in jlist:
                            jl = j - 2 * pr
                            pidx = (0, 1, 4, 5)[jl * 2 + half]
                            php = psum[pidx][:, 0:512]
                            bp = bg(f"ps{pidx}")
                            for c in range(NCH):
                                lh = xn[:, c * TB + j * 128: c * TB + (j + 1) * 128]
                                rh = wsl[:, c * 512:(c + 1) * 512]
                                em.op("pe", MM(php, lh, rh, c == 0, False),
                                      reads=[bxn, bw], writes=[bp], last=False)
                            bo = bias_roff[pname]
                            em.op("pe", MM(php, ones_b[0:1, 0:128],
                                           qkvb[:, bo + o0: bo + o0 + 512], False, True),
                                  reads=[bg("ones_b"), bg("qkvb")], writes=[bp], last=True)
                            if pname == "v":
                                d1 = vdup_c[:, jl * 2 * DM + o0: jl * 2 * DM + o0 + 512]
                                d2 = vdup_c[:, jl * 2 * DM + DM + o0: jl * 2 * DM + DM + o0 + 512]
                                bdst = bg("vdup")
                                em.op("act", lambda e, d=d1, p=php: e.activation(out=d, in_=p, func=AF.Copy),
                                      reads=[bp], writes=[bdst])
                                em.op("act", lambda e, d=d2, s=d1: e.activation(out=d, in_=s, func=AF.Copy),
                                      reads=[bdst], writes=[bdst])
                            else:
                                r_s = elus[:, 0:512]
                                e_s = elus[:, 512:1024]
                                bel = bg("elus")
                                em.op("act", lambda e, d=r_s, p=php: e.activation(
                                    out=d, in_=p, func=AF.Relu, scale=-1.0),
                                    reads=[bp], writes=[bel])
                                em.op("act", lambda e, d=e_s, s=r_s: e.activation(
                                    out=d, in_=s, func=AF.Exp, scale=-1.0),
                                    reads=[bel], writes=[bel])
                                if pname == "q":
                                    dd = qtm_c[:, jl * DM + o0: jl * DM + o0 + 512]
                                    bdst = bg("qtm")
                                else:
                                    dd = kdup_c[:, jl * 2 * DM + o0: jl * 2 * DM + o0 + 512]
                                    bdst = bg("kdup")
                                em.op("dve", lambda e, d=dd, s=e_s, p=php: e.scalar_tensor_tensor(
                                    out=d, in0=p, scalar=0.0, in1=s, op0=ALU.max, op1=ALU.add),
                                    reads=[bp, bel], writes=[bdst])
                                if pname == "k":
                                    d2 = kdup_c[:, jl * 2 * DM + DM + o0: jl * 2 * DM + DM + o0 + 512]
                                    em.op("act", lambda e, d=d2, s=dd: e.activation(
                                        out=d, in_=s, func=AF.Copy),
                                        reads=[bdst], writes=[bdst])
            return unit

        def make_D_units(blk, j, pr):
            t0 = blk * TB
            jl = j - 2 * pr
            q_ap = qtm_c[:, jl * DM:(jl + 1) * DM]
            k_ap = kdup_c[:, jl * 2 * DM:(jl + 1) * 2 * DM]
            v_ap = vdup_c[:, jl * 2 * DM:(jl + 1) * 2 * DM]

            def a_q(hq):
                def unit():
                    ba, bp3 = bg("abuf"), bg("p3b")
                    bq, bk = bg("qtm"), bg("kdup")
                    ho = hq * HQ
                    em.op("dve", lambda e, q=q_ap, k=k_ap, o=ho: e.tensor_tensor(
                        out=mkap(p3b_t, [[HQ * D, R], [1, HQ * D]]),
                        in0=mkap(q, [[0, R], [1, HQ * D]], off=o * D),
                        in1=mkap(k, [[D, R], [1, HQ * D]], off=o * D),
                        op=ALU.mult),
                        reads=[bq, bk], writes=[bp3])
                    em.op("dve", lambda e, o=ho: e.tensor_reduce(
                        out=mkap(abuf[:], [[H, R], [1, HQ]], off=o),
                        in_=mkap(p3b_t, [[HQ * D, R], [D, HQ], [1, D]]),
                        axis=AX.X, op=ALU.add),
                        reads=[bp3], writes=[ba])
                return unit

            def a_tail():
                def unit():
                    ba, bab, bc = bg("abuf"), bg("abb"), bg("csc")
                    em.op("dve", lambda e: e.tensor_reduce(
                        out=mkap(csc[:], [[1, H]]),
                        in_=mkap(abuf[:], [[1, H], [H, R]]),
                        axis=AX.X, op=ALU.add),
                        reads=[ba], writes=[bc])
                    em.op("dve", lambda e: e.reciprocal(out=csc[:, 0:H], in_=csc[:, 0:H]),
                          reads=[bc], writes=[bc])
                    em.op("act", lambda e: e.activation(out=abb[:], in_=abuf[:], func=AF.Copy),
                          reads=[ba], writes=[bab])
                return unit

            def av_q(hq):
                def unit():
                    bab, bav, bv, bp3 = bg("abb"), bg("avb"), bg("vdup"), bg("p3b")
                    ho = hq * HQ
                    em.op("dve", lambda e, v=v_ap, o=ho: e.tensor_tensor(
                        out=mkap(p3b_t, [[D * R, HQ], [R, D], [1, R]]),
                        in0=mkap(abb[:], [[1, HQ], [0, D], [H, R]], off=o),
                        in1=mkap(v, [[D, HQ], [1, D], [D, R]], off=o * D),
                        op=ALU.mult),
                        reads=[bab, bv], writes=[bp3])
                    em.op("dve", lambda e, o=ho: e.tensor_reduce(
                        out=mkap(avb[:], [[D, HQ], [1, D]], off=o * D),
                        in_=mkap(p3b_t, [[D * R, HQ], [R, D], [1, R]]),
                        axis=AX.X, op=ALU.add),
                        reads=[bp3], writes=[bav])
                return unit

            def d_tail():
                def unit():
                    bav, bc = bg("avb"), bg("csc")
                    asl = j % 2
                    a_ap = atm[:, asl * DM:(asl + 1) * DM]
                    bat = bg(f"atm{asl}")
                    em.op("dve", lambda e, d=a_ap: e.tensor_tensor(
                        out=mkap(d, [[D, H], [1, D]]),
                        in0=mkap(avb[:], [[D, H], [1, D]]),
                        in1=mkap(csc[:], [[1, H], [0, D]]),
                        op=ALU.mult),
                        reads=[bav, bc], writes=[bat])
                    em.dma(f"sat{asl}", lambda e, d=a_ap, jj=j, tt=t0: e.dma_start(
                        out=s_at[tt + jj * 128: tt + (jj + 1) * 128, :], in_=d),
                        reads=[bat], writes=[bg(f"s_at{asl}")])
                return unit

            return ([a_q(h) for h in range(4)] + [a_tail()] +
                    [av_q(h) for h in range(4)] + [d_tail()])

        def make_gu_unit(blk, ei, mat, s_src, half):
            def unit():
                sl = pool_seq[0] % 3
                pool_seq[0] += 1
                dst = pool_c[sl]
                bw = bg(f"pool{sl}")
                em.dma(f"lpool{sl}", lambda e, d=dst, s=s_src, i=ei, o=half * 512: e.dma_start(
                    out=d.rearrange("p (c n) -> p c n", c=NCH, n=512),
                    in_=s[i].rearrange("(c p) n -> p c n", c=NCH, p=128)[:, :, o:o + 512]),
                    reads=[bg(f"s_{mat}{ei}")], writes=[bw])
                by = bg("y")
                for oc in range(4):
                    co = half * 4 + oc
                    pidx = (0, 1, 4, 5)[co % 4]
                    php = psum[pidx][:, 0:TB]
                    bp = bg(f"ps{pidx}")
                    for c in range(NCH):
                        em.op("pe", MM(php, dst[:, c * 512 + oc * 128: c * 512 + (oc + 1) * 128],
                                       y_c[:, c * TB:(c + 1) * TB], c == 0, c == NCH - 1),
                              reads=[bw, by], writes=[bp], last=(c == NCH - 1))
                    mc = mid_c[:, co * TB:(co + 1) * TB]
                    if mat == "g":
                        em.op("act", lambda e, d=mc, p=php: e.activation(out=d, in_=p, func=AF.Silu),
                              reads=[bp], writes=[bg("mid")])
                    else:
                        usb = sqb[:, (co % 4) * TB:(co % 4 + 1) * TB]
                        bub = bg(f"sqb{co % 4}")
                        em.op("act", lambda e, d=usb, p=php: e.activation(out=d, in_=p, func=AF.Copy),
                              reads=[bp], writes=[bub])
                        em.op("pool", lambda e, d=mc, s=usb: e.tensor_tensor(
                            out=d, in0=d, in1=s, op=ALU.mult),
                            reads=[bub, bg("mid")], writes=[bg("mid")])
            return unit

        def make_down_unit(blk, ei, half):
            def unit():
                sl = pool_seq[0] % 3
                pool_seq[0] += 1
                dst = pool_c[sl]
                bw = bg(f"pool{sl}")
                em.dma(f"lpool{sl}", lambda e, d=dst, i=ei, o=half * 512: e.dma_start(
                    out=d.rearrange("p (c n) -> p c n", c=NCH, n=512),
                    in_=s_d[i].rearrange("(c p) n -> p c n", c=NCH, p=128)[:, :, o:o + 512]),
                    reads=[bg(f"s_d{ei}")], writes=[bw])
                o0 = half * 512
                for j in range(ntiles):
                    pidx = (2, 6)[j % 2] + half
                    php = psum[pidx][:, 0:512]
                    bp = bg(f"ps{pidx}")
                    for c in range(NCH):
                        em.op("pe", MM(php, mid_c[:, c * TB + j * 128: c * TB + (j + 1) * 128],
                                       dst[:, c * 512:(c + 1) * 512], c == 0, c == NCH - 1),
                              reads=[bg("mid"), bw], writes=[bp], last=(c == NCH - 1))
                    rsl = routed[:, j * DM + o0: j * DM + o0 + 512]
                    if ei == 0:
                        em.op("dve", lambda e, d=rsl, p=php, jj=j, i=ei: e.tensor_scalar(
                            out=d, in0=p, scalar1=comb[:, jj * NE + i: jj * NE + i + 1],
                            scalar2=None, op0=ALU.mult),
                            reads=[bp, bg("comb")], writes=[bg("routed")])
                    else:
                        em.op("dve", lambda e, d=rsl, p=php, jj=j, i=ei: e.scalar_tensor_tensor(
                            out=d, in0=p, scalar=comb[:, jj * NE + i: jj * NE + i + 1],
                            in1=d, op0=ALU.mult, op1=ALU.add),
                            reads=[bp, bg("comb"), bg("routed")], writes=[bg("routed")])
            return unit

        def make_lat_chunk(blk, c):
            def unit():
                t0 = blk * TB
                if c == 0:
                    em.barrier([bg("mid")], [bg("attnf")])
                em.dma("lat", lambda e, cc=c, tt=t0: e.dma_start(
                    out=attnf_c[:, cc * TB:(cc + 1) * TB],
                    in_=s_at[tt:tt + TB, cc * 128:(cc + 1) * 128], transpose=True),
                    reads=[bg("s_at0"), bg("s_at1")],
                    war=([bg("attnf")] if c == 0 else []), queue="act")
                if c == NCH - 1:
                    em.seal("lat", [bg("attnf")])
            return unit

        def make_shared_gu_unit(blk, qt):
            def unit():
                by = bg("y")
                slots = {}
                for nm, s_src in (("sg", s_sg), ("su", s_su)):
                    sl = pool_seq[0] % 3
                    pool_seq[0] += 1
                    slots[nm] = (pool_c[sl], bg(f"pool{sl}"))
                    em.dma(f"lpool{sl}", lambda e, d=pool_c[sl], s=s_src, q=qt: e.dma_start(
                        out=d.rearrange("p (c n) -> p c n", c=NCH, n=512),
                        in_=s.rearrange("(c p) n -> p c n", c=NCH, p=128)[:, :, q * 512:(q + 1) * 512]),
                        reads=[bg("s_" + nm)], writes=[bg(f"pool{sl}")])
                for nm in ("sg", "su"):
                    dst, bw = slots[nm]
                    for oc in range(4):
                        pidx = (0, 1, 4, 5)[oc % 4]
                        php = psum[pidx][:, 0:TB]
                        bp = bg(f"ps{pidx}")
                        for c in range(NCH):
                            em.op("pe", MM(php, dst[:, c * 512 + oc * 128: c * 512 + (oc + 1) * 128],
                                           y_c[:, c * TB:(c + 1) * TB], c == 0, c == NCH - 1),
                                  reads=[bw, by], writes=[bp], last=(c == NCH - 1))
                        mc = mids_c[:, oc * TB:(oc + 1) * TB]
                        if nm == "sg":
                            em.op("act", lambda e, d=mc, p=php: e.activation(out=d, in_=p, func=AF.Silu),
                                  reads=[bp], writes=[bg("mids")])
                        else:
                            usb = sqb[:, (oc % 4) * TB:(oc % 4 + 1) * TB]
                            bub = bg(f"sqb{oc % 4}")
                            em.op("act", lambda e, d=usb, p=php: e.activation(out=d, in_=p, func=AF.Copy),
                                  reads=[bp], writes=[bub])
                            em.op("pool", lambda e, d=mc, s=usb: e.tensor_tensor(
                                out=d, in0=d, in1=s, op=ALU.mult),
                                reads=[bub, bg("mids")], writes=[bg("mids")])
            return unit

        def make_shared_down_unit(blk, qt):
            def unit():
                sl = pool_seq[0] % 3
                pool_seq[0] += 1
                dst = pool_c[sl]
                bw = bg(f"pool{sl}")
                em.dma(f"lpool{sl}", lambda e, d=dst, q=qt: e.dma_start(
                    out=d.rearrange("p (c n) -> p c n", c=4, n=DM),
                    in_=s_sd.rearrange("(c p) n -> p c n", c=FS // 128, p=128)[:, q * 4:(q + 1) * 4, :]),
                    reads=[bg("s_sd")], writes=[bw])
                for j in range(ntiles):
                    for half in range(2):
                        pidx = (2, 6)[j % 2] + half
                        php = psum[pidx][:, 0:512]
                        bp = bg(f"ps{pidx}")
                        o0 = half * 512
                        for c in range(4):
                            em.op("pe", MM(php, mids_c[:, c * TB + j * 128: c * TB + (j + 1) * 128],
                                           dst[:, c * DM + o0: c * DM + o0 + 512], c == 0, c == 3),
                                  reads=[bg("mids"), bw], writes=[bp], last=(c == 3))
                        rsl = routed[:, j * DM + o0: j * DM + o0 + 512]
                        em.op("dve", lambda e, d=rsl, p=php: e.tensor_tensor(
                            out=d, in0=d, in1=p, op=ALU.add),
                            reads=[bp, bg("routed")], writes=[bg("routed")])
            return unit

        def make_L_unit(blk, j):
            def unit():
                t0 = blk * TB
                bxr = bg("x0re")
                em.dma("lx0", lambda e, jj=j, tt=t0: e.dma_start(
                    out=x0re[:], in_=x_in[tt + jj * 128: tt + (jj + 1) * 128, :]),
                    reads=[], writes=[bxr])
                bout = bg("outsb")
                for half in range(2):
                    pidx = 2 + half
                    bp = bg(f"ps{pidx}")
                    for cc in range(4):
                        co = half * 4 + cc
                        em.op("pe", lambda e, o=psum[pidx][:, cc * 128:(cc + 1) * 128],
                              i_=zres[:, co * TB + j * 128: co * TB + (j + 1) * 128]: e.transpose(
                                  out=o, in_=i_, identity=eye[:]),
                              reads=[bg("zres"), bg("eye")], writes=[bp], last=(cc == 3))
                    osl = outsb[:, half * 512: half * 512 + 512]
                    em.op("dve", lambda e, d=osl, p=psum[pidx][:, 0:512], jj=j, h=half: e.tensor_tensor(
                        out=d, in0=p, in1=routed[:, jj * DM + h * 512: jj * DM + h * 512 + 512],
                        op=ALU.add),
                        reads=[bp, bg("routed")], writes=[bout])
                    em.op("dve", lambda e, d=osl, h=half: e.tensor_tensor(
                        out=d, in0=d, in1=x0re[:, h * 512: h * 512 + 512], op=ALU.add),
                        reads=[bxr, bout], writes=[bout])
                em.dma("sout", lambda e, jj=j, tt=t0: e.dma_start(
                    out=out_t[tt + jj * 128: tt + (jj + 1) * 128, :], in_=outsb[:]),
                    reads=[bg("outsb")], writes=[])
            return unit

        def build_tail(blk):
            units = []
            for ei in range(NE):
                for mat, s_src in (("g", s_g), ("u", s_u)):
                    for half in range(2):
                        units.append(make_gu_unit(blk, ei, mat, s_src, half))
                for half in range(2):
                    units.append(make_down_unit(blk, ei, half))
            for qt in range(FS // 512):
                units.append(make_shared_gu_unit(blk, qt))
                units.append(make_shared_down_unit(blk, qt))
            for j in range(ntiles):
                units.append(make_L_unit(blk, j))
            return units

        def emit_serial_tailstages(blk):
            bm = bcast((rows_b[:, 0:TB], rows_b[:, TB:2 * TB]), 4)
            bs = bcast((rows_b[:, 2 * TB:3 * TB], rows_b[:, 3 * TB:4 * TB]), 5)
            bz0, bz = bg("z0"), bg("zres")
            for c in range(NCH):
                zc = zres[:, c * TB:(c + 1) * TB]
                z0c = z0[:, c * TB:(c + 1) * TB]
                em.op("dve", lambda e, a=zc, b=z0c: e.tensor_tensor(
                    out=a, in0=b, in1=psum[4][:, 0:TB], op=ALU.subtract),
                    reads=[bz0, bm], writes=[bz])
                em.op("dve", lambda e, a=zc: e.tensor_tensor(
                    out=a, in0=a, in1=psum[5][:, 0:TB], op=ALU.mult),
                    reads=[bs, bz], writes=[bz])
            batf = bg("attnf")
            for half in range(2):
                wsl, bw = wqv_load(s_oT, "s_oT", half)
                for oc in range(4):
                    co = half * 4 + oc
                    pidx = (0, 1, 4, 5)[co % 4]
                    php = psum[pidx][:, 0:TB]
                    bp = bg(f"ps{pidx}")
                    for c in range(NCH):
                        lh = wsl[:, c * 512 + oc * 128: c * 512 + (oc + 1) * 128]
                        rh = attnf_c[:, c * TB:(c + 1) * TB]
                        em.op("pe", MM(php, lh, rh, c == 0, c == NCH - 1),
                              reads=[bw, batf], writes=[bp], last=(c == NCH - 1))
                    zc = zres[:, co * TB:(co + 1) * TB]
                    em.op("dve", lambda e, a=zc, p=php, cc=co: e.scalar_tensor_tensor(
                        out=a, in0=p, scalar=bvec[:, cc:cc + 1], in1=a, op0=ALU.add, op1=ALU.add),
                        reads=[bp, bg("bvec"), bg("zres")], writes=[bg("zres")])
            st2 = ln_stats(zres, bg("zres"), True, False)
            bm2 = bcast(st2["m"], 4)
            bs2_ = bcast(st2["s"], 5)
            for c in range(NCH):
                zc = zres[:, c * TB:(c + 1) * TB]
                xnc = xn[:, c * TB:(c + 1) * TB]
                em.op("dve", lambda e, a=tscr[:, 0:TB], b=zc: e.tensor_tensor(
                    out=a, in0=b, in1=psum[4][:, 0:TB], op=ALU.subtract),
                    reads=[bg("zres"), bm2], writes=[bg("tscr")])
                em.op("dve", lambda e, a=xnc, b=tscr[:, 0:TB], cc=c: e.scalar_tensor_tensor(
                    out=a, in0=b, scalar=lnv[:, 2 * NCH + cc: 2 * NCH + cc + 1],
                    in1=psum[5][:, 0:TB], op0=ALU.mult, op1=ALU.mult),
                    reads=[bg("tscr"), bs2_, bg("lnv")], writes=[bg("xn")])
                em.op("dve", lambda e, a=xnc, cc=c: e.tensor_scalar(
                    out=a, in0=a, scalar1=lnv[:, 3 * NCH + cc: 3 * NCH + cc + 1],
                    scalar2=None, op0=ALU.add),
                    reads=[bg("lnv")], writes=[bg("xn")])
            em.barrier(G_CD + G_E, G_G)
            for g in range(MLP // 512):
                slot = g % 2
                wsl = wfc_c[slot]
                bw = bg(f"wfc{slot}")
                em.dma(f"lwfc{slot}", lambda e, d=wsl, gg=g: e.dma_start(
                    out=d.rearrange("p (c n) -> p c n", c=NCH, n=512),
                    in_=s_fcT.rearrange("(c p) n -> p c n", c=NCH, p=128)[:, :, gg * 512:(gg + 1) * 512]),
                    reads=[bg("s_fcT")], writes=[bw])
                for oc in range(4):
                    co = g * 4 + oc
                    pidx = (0, 1, 4, 5)[co % 4]
                    php = psum[pidx][:, 0:TB]
                    bp = bg(f"ps{pidx}")
                    for c in range(NCH):
                        lh = wsl[:, c * 512 + oc * 128: c * 512 + (oc + 1) * 128]
                        rh = xn[:, c * TB:(c + 1) * TB]
                        em.op("pe", MM(php, lh, rh, c == 0, c == NCH - 1),
                              reads=[bw, bg("xn")], writes=[bp], last=(c == NCH - 1))
                    hc = hbuf_c[:, co * TB:(co + 1) * TB]
                    em.op("act", lambda e, d=hc, p=php, cc=co: e.activation(
                        out=d, in_=p, func=AF.Gelu, bias=bvec[:, 2 * NCH + cc: 2 * NCH + cc + 1]),
                        reads=[bp, bg("bvec")], writes=[bg("hbuf")])
            for g in range(4):
                slot = g % 2
                wsl = wpj_c[slot]
                bw = bg(f"wpj{slot}")
                em.dma(f"lwpj{slot}", lambda e, d=wsl, gg=g: e.dma_start(
                    out=d.rearrange("p (c n) -> p c n", c=MLP // 128, n=256),
                    in_=s_pjT.rearrange("(c p) n -> p c n", c=MLP // 128, p=128)[:, :, gg * 256:(gg + 1) * 256]),
                    reads=[bg("s_pjT")], writes=[bw])
                for oc in range(2):
                    co = g * 2 + oc
                    pidx = (0, 1, 4, 5)[co % 4]
                    php = psum[pidx][:, 0:TB]
                    bp = bg(f"ps{pidx}")
                    for c in range(MLP // 128):
                        lh = wsl[:, c * 256 + oc * 128: c * 256 + (oc + 1) * 128]
                        rh = hbuf_c[:, c * TB:(c + 1) * TB]
                        em.op("pe", MM(php, lh, rh, c == 0, c == MLP // 128 - 1),
                              reads=[bw, bg("hbuf")], writes=[bp], last=(c == MLP // 128 - 1))
                    zc = zres[:, co * TB:(co + 1) * TB]
                    em.op("dve", lambda e, a=zc, p=php, cc=co: e.scalar_tensor_tensor(
                        out=a, in0=p, scalar=bvec[:, NCH + cc: NCH + cc + 1], in1=a,
                        op0=ALU.add, op1=ALU.add),
                        reads=[bp, bg("bvec"), bg("zres")], writes=[bg("zres")])
            b7, b6 = bg("ps7"), bg("ps6")
            for c in range(NCH):
                ss = c % 2
                xb = sqb[:, ss * TB:(ss + 1) * TB]
                zc = zres[:, c * TB:(c + 1) * TB]
                z0c = z0[:, c * TB:(c + 1) * TB]
                em.op("dve", lambda e, d=xb, a=zc, b=z0c: e.tensor_tensor(
                    out=d, in0=a, in1=b, op=ALU.add),
                    reads=[bg("zres"), bg("z0")], writes=[bg(f"sqb{ss}")])
                sq = sqb[:, (2 + ss) * TB:(3 + ss) * TB]
                em.op("dve", lambda e, d=sq, s=xb: e.tensor_tensor(
                    out=d, in0=s, in1=s, op=ALU.mult),
                    reads=[bg(f"sqb{ss}")], writes=[bg(f"sqb{2 + ss}")])
                em.op("pe", MM(psum[7][0:1, 0:TB], onec_b[:], xb, c == 0, c == NCH - 1),
                      reads=[bg("onec_b"), bg(f"sqb{ss}")], writes=[b7], last=True)
                em.op("pe", MM(psum[6][0:1, 0:TB], onec_b[:], sq, c == 0, c == NCH - 1),
                      reads=[bg("onec_b"), bg(f"sqb{2 + ss}")], writes=[b6], last=True)
            rf, rb = bg("rows_f"), bg("rows_b")
            m_r, e2_r, s_r = rows_f[:, 0:TB], rows_f[:, TB:2 * TB], rows_f[:, 2 * TB:3 * TB]
            em.op("dve", lambda e: e.tensor_scalar(out=m_r, in0=psum[7][0:1, 0:TB], scalar1=INV_DM,
                                                   scalar2=None, op0=ALU.mult), reads=[b7], writes=[rf])
            em.op("dve", lambda e: e.tensor_scalar(out=e2_r, in0=psum[6][0:1, 0:TB], scalar1=INV_DM,
                                                   scalar2=None, op0=ALU.mult), reads=[b6], writes=[rf])
            em.op("dve", lambda e: e.tensor_tensor(out=s_r, in0=m_r, in1=m_r, op=ALU.mult),
                  reads=[rf], writes=[rf])
            em.op("dve", lambda e: e.tensor_tensor(out=s_r, in0=e2_r, in1=s_r, op=ALU.subtract),
                  reads=[rf], writes=[rf])
            em.op("act", lambda e: e.activation(out=s_r, in_=s_r, func=AF.Sqrt, bias=epsr[:, 0:1]),
                  reads=[rf, bg("epsr")], writes=[rf])
            em.op("dve", lambda e: e.reciprocal(out=s_r, in_=s_r), reads=[rf], writes=[rf])
            rp = {}
            for (snm, bo, row) in (("m", 0, m_r), ("s", 2, s_r)):
                hi = rows_b[:, bo * TB:(bo + 1) * TB]
                lo = rows_b[:, (bo + 1) * TB:(bo + 2) * TB]
                em.op("dve", lambda e, r=row, h_=hi: e.tensor_copy(out=h_, in_=r),
                      reads=[rf], writes=[rb])
                em.op("dve", lambda e, r=row, h_=hi, l_=lo: e.tensor_tensor(
                    out=l_, in0=r, in1=h_, op=ALU.subtract),
                    reads=[rf, rb], writes=[rb])
                rp[snm] = (hi, lo)
            bm3 = bcast(rp["m"], 4)
            bs3_ = bcast(rp["s"], 5)
            for c in range(NCH):
                zc = zres[:, c * TB:(c + 1) * TB]
                z0c = z0[:, c * TB:(c + 1) * TB]
                yc = y_c[:, c * TB:(c + 1) * TB]
                em.op("dve", lambda e, a=tscr[:, 0:TB], b=zc, c0=z0c: e.tensor_tensor(
                    out=a, in0=b, in1=c0, op=ALU.add),
                    reads=[bg("zres"), bg("z0")], writes=[bg("tscr")])
                em.op("dve", lambda e, a=tscr[:, 0:TB]: e.tensor_tensor(
                    out=a, in0=a, in1=psum[4][:, 0:TB], op=ALU.subtract),
                    reads=[bg("tscr"), bm3], writes=[bg("tscr")])
                em.op("dve", lambda e, a=yc, b=tscr[:, 0:TB]: e.tensor_tensor(
                    out=a, in0=b, in1=psum[5][:, 0:TB], op=ALU.mult),
                    reads=[bg("tscr"), bs3_], writes=[bg("y")])
            for j in range(ntiles):
                bp = bg("ps2")
                for c in range(NCH):
                    lh = y_c[:, c * TB + j * 128: c * TB + (j + 1) * 128]
                    rh = wrt[:, c * NE:(c + 1) * NE]
                    em.op("pe", MM(psum[2][:, 0:NE], lh, rh, c == 0, c == NCH - 1),
                          reads=[bg("y"), bg("wrt")], writes=[bp], last=(c == NCH - 1))
                bc, bcm = bg("csc"), bg("comb")
                cj = comb[:, j * NE:(j + 1) * NE]
                em.op("dve", lambda e: e.tensor_reduce(out=csc[:, 8:9], in_=psum[2][:, 0:NE],
                                                       axis=AX.X, op=ALU.max, negate=True),
                      reads=[bp], writes=[bc])
                em.op("act", lambda e, d=cj: e.activation(out=d, in_=psum[2][:, 0:NE], func=AF.Exp,
                                                          bias=csc[:, 8:9], accum_out=csc[:, 9:10]),
                      reads=[bp, bc], writes=[bcm, bc])
                em.op("dve", lambda e: e.reciprocal(out=csc[:, 10:11], in_=csc[:, 9:10]),
                      reads=[bc], writes=[bc])
                em.op("dve", lambda e, d=cj: e.tensor_scalar(
                    out=csc[:, 0:8], in0=d, scalar1=1.0, scalar2=None, op0=ALU.is_equal),
                    reads=[bcm], writes=[bc])
                em.op("dve", lambda e, d=cj: e.scalar_tensor_tensor(
                    out=csc[:, 0:8], in0=csc[:, 0:8], scalar=-3.0, in1=d, op0=ALU.mult, op1=ALU.add),
                    reads=[bc, bcm], writes=[bc])
                em.op("dve", lambda e: e.tensor_reduce(out=csc[:, 12:13], in_=csc[:, 0:8],
                                                       axis=AX.X, op=ALU.max),
                      reads=[bc], writes=[bc])
                em.op("dve", lambda e, d=cj: e.tensor_scalar(
                    out=csc[:, 0:8], in0=d, scalar1=csc[:, 12:13], scalar2=None, op0=ALU.is_ge),
                    reads=[bcm, bc], writes=[bc])
                em.op("dve", lambda e, d=cj: e.scalar_tensor_tensor(
                    out=d, in0=d, scalar=csc[:, 10:11], in1=csc[:, 0:8], op0=ALU.mult, op1=ALU.mult),
                    reads=[bc, bcm], writes=[bcm])

        tail_units = []
        for blk in range(nblocks):
            em.barrier(G_G + G_E, G_CD)
            st1 = emit_A(blk)
            emit_Bp(blk, st1)
            head_units = []
            npairs = (ntiles + 1) // 2
            for pr in range(npairs):
                head_units.append(make_C_pair(blk, pr))
                for j in (2 * pr, 2 * pr + 1):
                    if j < ntiles:
                        head_units.extend(make_D_units(blk, j, pr))
            nj = min(48, len(tail_units))
            ti = 0
            nh_ = len(head_units)
            for i, hu in enumerate(head_units):
                hu()
                if blk == 0 and i == 21:
                    deferred_prep()
                take = ((i + 1) * nj) // nh_ - ti
                for _ in range(take):
                    tail_units[ti]()
                    ti += 1
            while ti < nj:
                tail_units[ti]()
                ti += 1
            lat_chunks = [make_lat_chunk(blk, c) for c in range(NCH)]
            li = 0
            while ti < len(tail_units):
                if li < NCH:
                    lat_chunks[li]()
                    li += 1
                tail_units[ti]()
                ti += 1
            while li < NCH:
                lat_chunks[li]()
                li += 1
            emit_serial_tailstages(blk)
            tail_units = build_tail(blk)
        em.barrier(G_G + G_E, G_CD)
        for u in tail_units:
            u()

        st_out = em.stream("sout")
        em.sp_wait((st_out.sem, st_out.cnt))

        with nc.Block() as block:
            @block.sync
            def _(sync):
                for f in em.sp_q:
                    f(sync)

            @block.vector
            def _(vector):
                for f in em.engs["dve"].q:
                    f(vector)

            @block.scalar
            def _(scalar):
                for f in em.engs["act"].q:
                    f(scalar)

            @block.tensor
            def _(tensor):
                for f in em.engs["pe"].q:
                    f(tensor)

            @block.gpsimd
            def _(gpsimd):
                for f in em.engs["pool"].q:
                    f(gpsimd)

    return nc


def kernel(**inputs):
    x = np.ascontiguousarray(np.asarray(inputs["L0_attn_x"], dtype=np.float32))
    BSH = x.shape
    xf = x.reshape(-1, DM)
    TC = xf.shape[0] // NCORES

    def arr(name):
        return np.ascontiguousarray(np.asarray(inputs[name], dtype=np.float32))

    row = lambda a: np.ascontiguousarray(np.asarray(a, dtype=np.float32).reshape(1, -1))
    common = {
        "qw": arr("L0_attn_q_w"), "kw": arr("L0_attn_k_w"), "vw": arr("L0_attn_v_w"),
        "ow": arr("L0_attn_o_w"), "fcw": arr("L0_attn_fc_w"), "pjw": arr("L0_attn_proj_w"),
        "qb": row(inputs["L0_attn_q_b"]), "kb": row(inputs["L0_attn_k_b"]),
        "vb": row(inputs["L0_attn_v_b"]), "ob": row(inputs["L0_attn_o_b"]),
        "fcb": row(inputs["L0_attn_fc_b"]), "pjb": row(inputs["L0_attn_proj_b"]),
        "l1w": row(inputs["L0_attn_ln1_w"]), "l1b": row(inputs["L0_attn_ln1_b"]),
        "l2w": row(inputs["L0_attn_ln2_w"]), "l2b": row(inputs["L0_attn_ln2_b"]),
        "rtw": arr("L0_ffn_router_w"),
        "gw": arr("L0_ffn_gate_w"), "uw": arr("L0_ffn_up_w"), "dw": arr("L0_ffn_down_w"),
        "sgw": arr("L0_ffn_shared_gate_w"), "suw": arr("L0_ffn_shared_up_w"),
        "sdw": arr("L0_ffn_shared_down_w"),
        "eye_in": np.eye(128, dtype=np.float32),
    }
    nc = build_nc(TC)
    in_maps = [dict(common, x=np.ascontiguousarray(xf[c * TC:(c + 1) * TC])) for c in range(NCORES)]
    res = run_bass_kernel_spmd(nc, in_maps, core_ids=list(range(NCORES)))
    out = np.concatenate([r["out"] for r in res.results], axis=0)
    return out.reshape(BSH)
